# version 42 (final)
# baseline (speedup 1.0000x reference)
import numpy as np
import ml_dtypes
from contextlib import ExitStack

import concourse.bass as bass
import concourse.mybir as mybir
from concourse.bass_utils import run_bass_kernel_spmd

F32 = mybir.dt.float32
BF16 = mybir.dt.bfloat16
F16 = mybir.dt.float16
AF = mybir.ActivationFunctionType
ALU = mybir.AluOpType
AX = mybir.AxisListType
BFNP = ml_dtypes.bfloat16

RC = 12582912.0
QMAX = 127.0
EPS = 1e-5
INV32 = 1.0 / 32.0

SEQ = 4
L = 2048
D = 1024
NCHUNK = 64
II = 2816
NJ = 22
NCORES = 8

SET_SIGMOID = 2
SET_LNEXP = 6
SET_SILU = 18


def _load_act_set(scalar, func):
    try:
        from concourse.hw_specs import get_activation_tables
        tabs = get_activation_tables(scalar.bass.m.arch)
        want = {AF.Sigmoid: "sigmoid_and_others", AF.Exp: "natural_log_exp_and_others",
                AF.Ln: "natural_log_exp_and_others", AF.Silu: "silu_and_others"}[func]
        idx = list(tabs.keys()).index(want)
    except Exception:
        idx = {AF.Sigmoid: SET_SIGMOID, AF.Exp: SET_LNEXP,
               AF.Ln: SET_LNEXP, AF.Silu: SET_SILU}[func]
    return scalar.add_instruction(
        mybir.InstLoadActFuncSet(
            name=scalar.bass.get_next_instruction_name(),
            act_func_set_id=idx, ins=[], outs=[]))


class Sch:
    ENGINES = ("sync", "tensor", "vector", "scalar", "gpsimd")

    def __init__(self, nc, es):
        self.nc = nc
        self.es = es
        self.q = {e: [] for e in self.ENGINES}
        self.cnt = {}
        self.sems = {}

    def sem(self, name):
        s = self.es.enter_context(self.nc.semaphore(name))
        self.sems[name] = s
        self.cnt[name] = 0
        return s

    def emit(self, eng, fn, waits=(), incs=()):
        iv = []
        for s, n in incs:
            self.cnt[s] += n
            iv.append((s, n))
        self.q[eng].append((fn, tuple(waits), tuple(iv)))

    def wait(self, eng, *waits):
        self.q[eng].append((None, tuple(waits), ()))

    def now(self, s):
        return self.cnt[s]

    def replay(self, block):
        def mk(engname):
            def body(engine):
                for fn, waits, incs in self.q[engname]:
                    for s, v in waits:
                        if v > 0:
                            engine.wait_ge(self.sems[s], v)
                    if fn is None:
                        continue
                    inst = fn(engine)
                    for s, n in incs:
                        inst = inst.then_inc(self.sems[s], n)
            return body
        block.sync(mk("sync"))
        block.tensor(mk("tensor"))
        block.vector(mk("vector"))
        block.scalar(mk("scalar"))
        block.gpsimd(mk("gpsimd"))


def build_program(debug=False):
    nc = bass.Bass("TRN2", target_bir_lowering=False)
    es = ExitStack()
    nc._keep_es = es

    x_h = nc.dram_tensor("x", [SEQ, L, D], F32, kind="ExternalInput").ap()
    wq_h = nc.dram_tensor("wq", [D, D], F32, kind="ExternalInput").ap()
    bq_h = nc.dram_tensor("bq", [D], F32, kind="ExternalInput").ap()
    wk_h = nc.dram_tensor("wk", [D, D], F32, kind="ExternalInput").ap()
    bk_h = nc.dram_tensor("bk", [D], F32, kind="ExternalInput").ap()
    wv_h = nc.dram_tensor("wv", [D, D], F32, kind="ExternalInput").ap()
    bv_h = nc.dram_tensor("bv", [D], F32, kind="ExternalInput").ap()
    wo_h = nc.dram_tensor("wo", [D, D], F32, kind="ExternalInput").ap()
    bo_h = nc.dram_tensor("bo", [D], F32, kind="ExternalInput").ap()
    cx_h = nc.dram_tensor("cx", [1], F32, kind="ExternalInput").ap()
    cd_h = nc.dram_tensor("cd", [1], F32, kind="ExternalInput").ap()
    wg_h = nc.dram_tensor("wg", [II, D], F32, kind="ExternalInput").ap()
    wu_h = nc.dram_tensor("wu", [II, D], F32, kind="ExternalInput").ap()
    wd_h = nc.dram_tensor("wd", [D, II], F32, kind="ExternalInput").ap()
    y_h = nc.dram_tensor("y", [SEQ, L, D], F32, kind="ExternalOutput").ap()

    dbg = {}
    if debug:
        for nm, shp, dt in (("d_xfm", [128, 2048], BF16), ("d_qht", [128, 1024], BF16),
                            ("d_qhts", [128, 1024], BF16),
                            ("d_kfm", [128, 2048], BF16), ("d_vfm", [128, 2048], BF16),
                            ("d_negc", [128, 1], F32), ("d_zzt", [128, 160], F32),
                            ("d_diag", [128, 16], F32), ("d_av", [128, 1024], BF16),
                            ("d_xres", [128, 1024], F32), ("d_xln", [128, 1024], F32),
                            ("d_gmax", [1, 1], F32), ("d_rx", [128, 1], F32),
                            ("d_xq", [128, 1024], BF16), ("d_sgi", [128, 22], F32),
                            ("d_hq", [128, 1024], BF16), ("d_wdt", [128, 1024], BF16),
                            ("d_rd", [128, 1], F32), ("d_sd", [128, 1], F32),
                            ("d_g2", [1, 1], F32), ("d_hp", [128, 512], F16),
                            ("d_y", [128, 1024], F32), ("d_xres2", [128, 1024], F32)):
            dbg[nm] = nc.dram_tensor(nm, shp, dt, kind="ExternalOutput").ap()

    xln1_h = nc.dram_tensor("xln1_s", [NCHUNK, 128, D], F32).ap()
    hp_h = nc.dram_tensor("hp_s", [16, NJ, 128, 512], F16).ap()
    wgT_h = nc.dram_tensor("wgT_s", [NJ, 128, 1024], BF16).ap()
    wuT_h = nc.dram_tensor("wuT_s", [NJ, 128, 1024], BF16).ap()
    wdT_h = nc.dram_tensor("wdT_s", [NJ, 128, 1024], BF16).ap()
    cc1i_h = nc.dram_tensor("cc1_in", [1], F32).ap()
    cc1o_h = nc.dram_tensor("cc1_out", [1], F32).ap()
    cc2i_h = nc.dram_tensor("cc2_in", [1], F32).ap()
    cc2o_h = nc.dram_tensor("cc2_out", [1], F32).ap()

    eye_f = nc.inline_tensor(np.eye(128, dtype=np.float32), "eye_f")
    eye_b = nc.inline_tensor(np.eye(128, dtype=BFNP), "eye_b")
    pswap_np = np.zeros((128, 128), dtype=BFNP)
    for p in range(128):
        pswap_np[(p + 64) % 128, p] = 1.0
    pswap = nc.inline_tensor(pswap_np, "pswap")
    sel_np = np.zeros((16, 16 * 64), dtype=BFNP)
    for fb in range(16):
        sel_np[fb, fb * 64:(fb + 1) * 64] = 1.0
    sel_c = nc.inline_tensor(sel_np, "sel_c")

    def sb(name, shape, dt):
        return es.enter_context(nc.sbuf_tensor(name, shape, dt))

    eyeF = sb("eyeF", [128, 128], F32)
    eyeB = sb("eyeB", [128, 128], BF16)
    pswp = sb("pswp", [128, 128], BF16)
    sel_sb = sb("sel_sb", [16, 16 * 64], BF16)
    ones_r = sb("ones_r", [1, 128], BF16)
    bo_row = sb("bo_row", [1, 1024], BF16)
    stg_f32 = sb("stg_f32", [1, 1024], F32)
    bqc = sb("bqc", [128, 8], F32)
    bkc = sb("bkc", [128, 8], F32)
    bvc = sb("bvc", [128, 8], F32)
    sgi = sb("sgi", [128, NJ], F32)
    sui = sb("sui", [128, NJ], F32)
    sg_eff = sb("sg_eff", [128, NJ], F32)
    su_eff = sb("su_eff", [128, NJ], F32)
    tiny = sb("tiny", [128, 32], F32)
    (T_SIGX, T_SIGD, T_G11, T_NEGC, T_MA, T_RX, T_SX, T_SD, T_GMAXRUN, T_HMAXRUN,
     T_S1A, T_S1B, T_S1, T_NEGMEAN, T_S2, T_VP, T_RSTD, T_TMP1, T_TMP2, T_RX2,
     T_RCP, T_RCN, T_GUR) = range(23)
    zzt = sb("zzt", [128, 160], F32)
    diag_tm = sb("diag_tm", [128, 16], F32)
    diagT_sb = sb("diagT_sb", [16, 128], BF16)
    negc_col = [sb("negc_col0", [128, 1], F32), sb("negc_col1", [128, 1], F32)]
    rx_col = sb("rx_col", [128, 1], F32)
    sx_col = sb("sx_col", [128, 1], F32)
    rd_col = sb("rd_col", [128, 1], F32)
    sd_col = sb("sd_col", [128, 1], F32)

    wq_sb = sb("wq_sb", [128, 8 * 1024], BF16)
    wk_sb = sb("wk_sb", [128, 8 * 1024], BF16)
    wv_sb = sb("wv_sb", [128, 8 * 1024], BF16)
    wo_sb = sb("wo_sb", [128, 8 * 1024], BF16)
    x_tm = [sb("x_tm0", [128, 2048], F32), sb("x_tm1", [128, 2048], F32)]
    xfm = [sb("xfm0", [128, 8 * 256], BF16), sb("xfm1", [128, 8 * 256], BF16)]
    qht = [[sb("qht00", [128, 1024], BF16), sb("qht01", [128, 1024], BF16)],
           [sb("qht10", [128, 1024], BF16), sb("qht11", [128, 1024], BF16)]]
    qhts = [[sb("qhts00", [128, 1024], BF16), sb("qhts01", [128, 1024], BF16)],
            [sb("qhts10", [128, 1024], BF16), sb("qhts11", [128, 1024], BF16)]]
    kfm = [sb("kfm0", [128, 8 * 256], BF16), sb("kfm1", [128, 8 * 256], BF16)]
    vfm = [sb("vfm0", [128, 8 * 256], BF16), sb("vfm1", [128, 8 * 256], BF16)]
    e_sb = [sb("e_sb0", [128, 1024], BF16), sb("e_sb1", [128, 1024], BF16)]
    avfm = sb("avfm", [128, 1024], BF16)
    xres = sb("xres", [128, 1024], F32)
    junkb = sb("junkb", [128, 128], BF16)
    xln_buf = [sb("xln_buf0", [128, 1024], F32), sb("xln_buf1", [128, 1024], F32)]

    wraw = [sb("wraw0", [128, 2816], F32), sb("wraw1", [128, 2816], F32)]
    wscr = sb("wscr", [128, 2816], F32)
    wint = sb("wint", [128, 2816], BF16)
    tstage = sb("tstage", [128, 1024], BF16)

    wgT_sb = [sb("wgT_sb0", [128, 1024], BF16), sb("wgT_sb1", [128, 1024], BF16)]
    wuT_sb = [sb("wuT_sb0", [128, 1024], BF16), sb("wuT_sb1", [128, 1024], BF16)]
    sil_sb = [sb("sil_sb0", [128, 512], F32), sb("sil_sb1", [128, 512], F32)]

    xq_tm = sb("xq_tm", [128, 1024], BF16)

    def addr(t):
        return nc.lookup_mloc(t).addr

    def sb_at(name, shape, dt, off):
        return nc.alloc_sbuf_tensor_at(name, shape, dt, offset=off)

    assert addr(wk_sb) == addr(wq_sb) + 16384 and addr(wv_sb) == addr(wq_sb) + 32768
    wdT_sb = sb_at("wdT_sb", [128, NJ * 1024], BF16, addr(wq_sb))
    assert addr(x_tm[0]) == addr(wo_sb) + 16384
    hqT = sb_at("hqT", [128, NJ * 1024], BF16, addr(wo_sb))
    assert addr(qht[1][0]) + 2048 - addr(wo_sb) >= NJ * 2048
    hp_in = [sb_at("hp_in0", [128, 1024], F16, addr(kfm[0])),
             sb_at("hp_in1", [128, 1024], F16, addr(kfm[0]) + 2048)]
    q_scr = [sb_at("q_scr0", [128, 1024], F32, addr(vfm[0])),
             sb_at("q_scr1", [128, 1024], F32, addr(vfm[1]))]
    junkf = sb_at("junkf", [128, 1024], F32, addr(wint))
    hqT2 = sb_at("hqT2", [128, NJ * 1024], BF16, addr(wraw[0]))
    assert addr(wgT_sb[1]) + 2048 - addr(wraw[0]) >= NJ * 2048
    xl_b = [sb_at("xl_b0", [128, 1024], F32, addr(xln_buf[0])),
            sb_at("xl_b1", [128, 1024], F32, addr(xln_buf[1]))]
    hp_st = [sb_at("hp_st0", [128, 512], F16, addr(avfm)),
             sb_at("hp_st1", [128, 512], F16, addr(avfm) + 1024),
             sb_at("hp_st2", [128, 512], F16, addr(xres)),
             sb_at("hp_st3", [128, 512], F16, addr(xres) + 1024)]
    assert addr(xln_buf[1]) == addr(xln_buf[0]) + 4096
    y_buf = [sb_at("y_buf0", [128, 1024], F32, addr(wuT_sb[0])),
             sb_at("y_buf1", [128, 1024], F32, addr(sil_sb[0]))]
    assert addr(wuT_sb[1]) == addr(wuT_sb[0]) + 2048
    assert addr(sil_sb[1]) == addr(sil_sb[0]) + 2048
    assert addr(wraw[1]) == addr(wraw[0]) + 11264
    xqT = [sb_at("xqT0", [128, 8 * 1024], BF16, addr(wraw[0])),
           sb_at("xqT1", [128, 8 * 1024], BF16, addr(wraw[0]) + 16384)]
    assert addr(wraw[0]) + 32768 <= addr(wint)
    assert addr(wint) + 5632 - addr(wraw[0]) >= 32768

    PS = es.enter_context(nc.psum_tensor("PS", [128, 4096], F32))
    PSB = PS.bitcast(BF16)

    sch = Sch(nc, es)
    for s in ("dx", "dw", "dwst", "dst1", "dlx", "dwg", "dhp", "dhl", "dout", "dmsc",
              "t", "v", "s", "sd", "cc", "g"):
        sch.sem(s)
    E = sch.emit
    now = sch.now

    def DR(eng):
        sch.emit(eng, lambda e: e.drain())

    E("sync", lambda e: e.dma_start(out=eyeF[:, :], in_=eye_f.ap()), incs=[("dmsc", 16)])
    E("sync", lambda e: e.dma_start(out=eyeB[:, :], in_=eye_b.ap()), incs=[("dmsc", 16)])
    E("sync", lambda e: e.dma_start(out=pswp[:, :], in_=pswap.ap()), incs=[("dmsc", 16)])
    E("sync", lambda e: e.dma_start(out=sel_sb[:, :], in_=sel_c.ap()), incs=[("dmsc", 16)])
    for g in range(8):
        E("sync", lambda e, g=g: e.dma_start(out=bqc[:, g:g + 1], in_=bq_h[g * 128:(g + 1) * 128]), incs=[("dmsc", 16)])
        E("sync", lambda e, g=g: e.dma_start(out=bkc[:, g:g + 1], in_=bk_h[g * 128:(g + 1) * 128]), incs=[("dmsc", 16)])
        E("sync", lambda e, g=g: e.dma_start(out=bvc[:, g:g + 1], in_=bv_h[g * 128:(g + 1) * 128]), incs=[("dmsc", 16)])
    E("sync", lambda e: e.dma_start(out=stg_f32[:, :], in_=bo_h[:]), incs=[("dmsc", 16)])
    E("sync", lambda e: e.dma_start(out=tiny[0:1, T_SIGX:T_SIGX + 1], in_=cx_h[:]), incs=[("dmsc", 16)])
    E("sync", lambda e: e.dma_start(out=tiny[0:1, T_SIGD:T_SIGD + 1], in_=cd_h[:]), incs=[("dmsc", 16)])
    dmsc_all = now("dmsc")
    for _eng in ("tensor", "vector", "scalar", "gpsimd"):
        sch.wait(_eng, ("dmsc", dmsc_all))

    E("vector", lambda e: e.memset(ones_r[:, :], 1.0))
    E("vector", lambda e: e.tensor_copy(bo_row[:, :], stg_f32[:, :]),
      waits=[("dmsc", dmsc_all)])
    E("vector", lambda e: e.memset(tiny[:, T_GMAXRUN:T_GMAXRUN + 1], 0.0))
    E("vector", lambda e: e.memset(tiny[:, T_RCP:T_RCP + 1], RC))
    E("vector", lambda e: e.memset(tiny[:, T_RCN:T_RCN + 1], -RC))
    E("vector", lambda e: e.memset(tiny[:, T_HMAXRUN:T_HMAXRUN + 1], 0.0), incs=[("v", 1)])

    E("scalar", lambda e: _load_act_set(e, AF.Sigmoid))
    E("scalar", lambda e: e.activation(tiny[0:1, T_SIGX:T_SIGX + 1], tiny[0:1, T_SIGX:T_SIGX + 1], AF.Sigmoid),
      waits=[("dmsc", dmsc_all)])
    E("scalar", lambda e: e.activation(tiny[0:1, T_SIGD:T_SIGD + 1], tiny[0:1, T_SIGD:T_SIGD + 1], AF.Sigmoid),
      incs=[("s", 1)])
    E("scalar", lambda e: _load_act_set(e, AF.Exp), incs=[("s", 1)])
    sig_done = now("s")

    sch._wraw_free = {0: ("v", 0), 1: ("v", 0)}
    for (w_h, w_sb) in ((wq_h, wq_sb), (wk_h, wk_sb), (wv_h, wv_sb), (wo_h, wo_sb)):
        for j in range(8):
            E("sync", lambda e, w_h=w_h, j=j: e.dma_start(
                out=wraw[j % 2][:, 0:1024], in_=w_h[j * 128:(j + 1) * 128, :]),
              waits=[sch._wraw_free[j % 2]], incs=[("dw", 16)])
            E("vector", lambda e, w_sb=w_sb, j=j: e.tensor_copy(
                w_sb[:, j * 1024:(j + 1) * 1024], wraw[j % 2][:, 0:1024]),
              waits=[("dw", now("dw"))], incs=[("v", 1)])
            sch._wraw_free[j % 2] = ("v", now("v"))

    prep_evac = {0: 0, 1: 0}
    sch._tstage_free = ("dwst", 0)

    bank6_prep_ref = [None]

    def prep_gu_tile(w_h, s_sb, out_h, j, part):
        if part == 0:
            E("sync", lambda e, w_h=w_h, j=j: e.dma_start(
                out=wraw[j % 2][:, 0:1024], in_=w_h[j * 128:(j + 1) * 128, :]),
              waits=[sch._wraw_free[j % 2]], incs=[("dw", 16)])
            dwn = now("dw")
            E("vector", lambda e, j=j: e.tensor_reduce(
                tiny[:, T_TMP1:T_TMP1 + 1], wraw[j % 2][:, 0:1024], AX.X, ALU.max,
                apply_absolute_value=True), waits=[("dw", dwn)])
            DR("vector")
            E("vector", lambda e, j=j, s_sb=s_sb: e.tensor_scalar(
                s_sb[:, j:j + 1], tiny[:, T_TMP1:T_TMP1 + 1], 1.0 / QMAX, None, ALU.mult))
            DR("vector")
            E("vector", lambda e, j=j, s_sb=s_sb: e.reciprocal(
                tiny[:, T_GUR:T_GUR + 1], s_sb[:, j:j + 1]))
            return
        if part == 1:
            DR("vector")
            E("vector", lambda e, j=j: e.tensor_scalar(
                wscr[:, 0:1024], wraw[j % 2][:, 0:1024], tiny[:, T_GUR:T_GUR + 1], None, ALU.mult),
              waits=[("t", now("t"))], incs=[("v", 1)])
            sch._wraw_free[j % 2] = ("v", now("v"))
            DR("vector")
            E("vector", lambda e: e.tensor_scalar(
                wscr[:, 0:1024], wscr[:, 0:1024], RC, RC, ALU.add, ALU.subtract))
            DR("vector")
            E("vector", lambda e: e.tensor_scalar(
                wint[:, 0:1024], wscr[:, 0:1024], QMAX, -QMAX - 1.0, ALU.min, ALU.max),
              incs=[("v", 1)])
            sch._gu_vq = now("v")
            return
        vq = sch._gu_vq
        b6 = bank6_prep_ref[0]
        for b in (0, 4):
            for hb in range(b, b + 4):
                pcol = 6144 + (hb % 4) * 256
                E("tensor", lambda e, hb=hb, pcol=pcol: e.transpose(
                    PSB[:, pcol:pcol + 128], wint[:, hb * 128:(hb + 1) * 128], eyeB[:, :]),
                  waits=[("v", vq), b6[0]] if hb == b else (),
                  incs=[("t", 1)] if hb == b + 3 else ())
            tv = now("t")
            for hb in range(b, b + 4):
                pcol = 6144 + (hb % 4) * 256
                wv_ = [("t", tv)] if hb == b else []
                if hb == 0:
                    wv_.append(sch._tstage_free)
                E("vector", lambda e, hb=hb, pcol=pcol: e.tensor_copy(
                    tstage[:, hb * 128:(hb + 1) * 128], PSB[:, pcol:pcol + 128]),
                  waits=wv_, incs=[("v", 1)] if hb == b + 3 else ())
            b6[0] = ("v", now("v"))
        E("gpsimd", lambda e, j=j, out_h=out_h: e.dma_start(
            out=out_h[j, :, :], in_=tstage[:, :]),
          waits=[("v", now("v"))], incs=[("dwst", 16)])
        sch._tstage_free = ("dwst", now("dwst"))

    GU_TILES = ([(wg_h, sgi, wgT_h, j) for j in range(NJ)]
                + [(wu_h, sui, wuT_h, j) for j in range(NJ)])

    sch._wd_slot_free = {k: ("dwst", 0) for k in range(8)}
    for hb in range(8):
        E("sync", lambda e, hb=hb: e.dma_start(
            out=wraw[hb % 2][:, :], in_=wd_h[hb * 128:(hb + 1) * 128, :]),
          waits=[sch._wraw_free[hb % 2]], incs=[("dw", 16)])
        dwn = now("dw")
        E("vector", lambda e, hb=hb: e.tensor_reduce(
            tiny[:, T_TMP1:T_TMP1 + 1], wraw[hb % 2][:, :], AX.X, ALU.max,
            apply_absolute_value=True), waits=[("dw", dwn), ("t", now("t"))])
        DR("vector")
        E("vector", lambda e: e.tensor_scalar(
            tiny[:, T_TMP2:T_TMP2 + 1], tiny[:, T_TMP1:T_TMP1 + 1], 1.0 / QMAX, None, ALU.mult))
        DR("vector")
        E("vector", lambda e: e.reciprocal(
            tiny[:, T_TMP1:T_TMP1 + 1], tiny[:, T_TMP2:T_TMP2 + 1]))
        DR("vector")
        E("vector", lambda e, hb=hb: e.tensor_scalar(
            wscr[:, :], wraw[hb % 2][:, :], tiny[:, T_TMP1:T_TMP1 + 1], None, ALU.mult),
          incs=[("v", 1)])
        sch._wraw_free[hb % 2] = ("v", now("v"))
        DR("vector")
        E("vector", lambda e: e.tensor_scalar(
            wscr[:, :], wscr[:, :], RC, RC, ALU.add, ALU.subtract))
        DR("vector")
        E("vector", lambda e: e.tensor_scalar(
            wscr[:, :], wscr[:, :], QMAX, -QMAX - 1.0, ALU.min, ALU.max))
        DR("vector")
        E("vector", lambda e: e.tensor_scalar(
            wint[:, :], wscr[:, :], tiny[:, T_TMP2:T_TMP2 + 1], None, ALU.mult),
          incs=[("v", 1)])
        vq = now("v")
        for j in range(NJ):
            slot = j % 2
            poff = 3072 + slot * 512
            E("tensor", lambda e, j=j, poff=poff: e.transpose(
                PSB[:, 2 * poff:2 * poff + 128], wint[:, j * 128:(j + 1) * 128], eyeB[:, :]),
              waits=[("v", vq), ("v", prep_evac[slot])], incs=[("t", 1)])
            E("vector", lambda e, j=j, poff=poff: e.tensor_copy(
                tstage[:, (j % 8) * 128:(j % 8) * 128 + 128], PSB[:, 2 * poff:2 * poff + 128]),
              waits=[("t", now("t")), sch._wd_slot_free[j % 8]], incs=[("v", 1)])
            prep_evac[slot] = now("v")
            E("gpsimd", lambda e, j=j, hb=hb: e.dma_start(
                out=wdT_h[j, :, hb * 128:(hb + 1) * 128], in_=tstage[:, (j % 8) * 128:(j % 8) * 128 + 128]),
              waits=[("v", now("v"))], incs=[("dwst", 16)])
            sch._wd_slot_free[j % 8] = ("dwst", now("dwst"))

    sch._tstage_free = ("dwst", now("dwst"))
    prep_v = now("v")
    prep_dw = now("dw")
    prep_dwst_ref = [now("dwst")]

    last_exp = {0: 0, 1: 0}
    last_diag = {0: 0, 1: 0}
    acc_free = {0: ("v", 0), 1: ("v", 0)}
    bank7_free = ("v", prep_v)
    bank6_last_evac = ("v", prep_v)
    sev_free = ("v", 0)
    zadd_v = 0
    xtm_free = {0: ("v", 0), 1: ("v", 0)}
    xln_free = {0: ("dst1", 0), 1: ("dst1", 0)}
    ACC = {0: 2048, 1: 2560}
    x_loaded = {}
    xfm_ready = {}
    negc_v = {}
    qkv_done = {}
    swap_done = {}
    qround = [0]

    def load_group(gi):
        buf = gi % 2
        for ci in range(2):
            c = 2 * gi + ci
            seq, pos = c // 16, (c % 16) * 128
            E("sync", lambda e, buf=buf, ci=ci, seq=seq, pos=pos: e.dma_start(
                out=x_tm[buf][:, ci * 1024:(ci + 1) * 1024],
                in_=x_h[seq, pos:pos + 128, :]),
              waits=[xtm_free[buf]], incs=[("dx", 16)])
        x_loaded[gi] = now("dx")

    bgq = []

    def bg_pop(n=1):
        for _ in range(n):
            if bgq:
                bgq.pop(0)()

    def bg_flush():
        while bgq:
            bgq.pop(0)()

    def emit_xt_batch(gi, ci, j0):
        nonlocal_state = None
        buf = gi % 2
        for j in range(j0, j0 + 4):
            slot = j % 4
            poff = 3072 + slot * 128
            E("tensor", lambda e, ci=ci, j=j, poff=poff, buf=buf: e.transpose(
                PS[:, poff:poff + 128],
                x_tm[buf][:, ci * 1024 + j * 128: ci * 1024 + (j + 1) * 128],
                eyeF[:, :]),
              waits=([("dx", x_loaded[gi]), bank6_last_evac_ref[0]]
                     if j == j0 else ()),
              incs=[("t", 1)] if j == j0 + 3 else ())
        tv = now("t")
        for j in range(j0, j0 + 4):
            slot = j % 4
            poff = 3072 + slot * 128
            E("vector", lambda e, ci=ci, j=j, poff=poff, buf=buf: e.tensor_copy(
                xfm[buf][:, j * 256 + ci * 128: j * 256 + (ci + 1) * 128],
                PS[:, poff:poff + 128]),
              waits=[("t", tv)] if j == j0 else (),
              incs=[("v", 1)] if j == j0 + 3 else ())
        bank6_last_evac_ref[0] = ("v", now("v"))

    bank6_last_evac_ref = [bank6_last_evac]
    bank6_prep_ref[0] = bank6_last_evac_ref

    def emit_qkv_round(gi, kind, idx):
        buf = gi % 2
        bank = qround[0] % 2
        qround[0] += 1
        aoff = ACC[bank]
        w0 = [("v", xfm_ready[gi]), acc_free[bank]]
        if kind == "q":
            for j in range(8):
                E("tensor", lambda e, j=j, idx=idx, aoff=aoff, buf=buf: e.matmul(
                    PS[0:64, aoff:aoff + 256],
                    wq_sb[:, j * 1024 + idx * 128: j * 1024 + idx * 128 + 64],
                    xfm[buf][:, j * 256:(j + 1) * 256],
                    start=(j == 0), stop=(j == 7), tile_position=(0, 0)),
                  waits=w0 if j == 0 else ())
                E("tensor", lambda e, j=j, idx=idx, aoff=aoff, buf=buf: e.matmul(
                    PS[64:128, aoff:aoff + 256],
                    wq_sb[:, j * 1024 + idx * 128 + 64: j * 1024 + (idx + 1) * 128],
                    xfm[buf][:, j * 256:(j + 1) * 256],
                    start=(j == 0), stop=(j == 7), tile_position=(0, 64)),
                  incs=[("t", 1)] if j == 7 else ())
        else:
            wsb = wk_sb if kind == "k" else wv_sb
            for j in range(8):
                E("tensor", lambda e, j=j, idx=idx, aoff=aoff, wsb=wsb, buf=buf: e.matmul(
                    PS[:, aoff:aoff + 256],
                    wsb[:, j * 1024 + idx * 128: j * 1024 + (idx + 1) * 128],
                    xfm[buf][:, j * 256:(j + 1) * 256],
                    start=(j == 0), stop=(j == 7)),
                  waits=w0 if j == 0 else (), incs=[("t", 1)] if j == 7 else ())
        tv = now("t")
        for ci in range(2):
            if kind == "q":
                fn = lambda e, ci=ci, idx=idx, aoff=aoff, buf=buf: e.tensor_scalar(
                    qht[buf][ci][:, idx * 128:(idx + 1) * 128],
                    PS[:, aoff + ci * 128: aoff + (ci + 1) * 128],
                    bqc[:, idx:idx + 1], INV32, ALU.add, ALU.mult)
            elif kind == "k":
                fn = lambda e, ci=ci, idx=idx, aoff=aoff, buf=buf: e.tensor_scalar(
                    kfm[buf][:, idx * 256 + ci * 128: idx * 256 + (ci + 1) * 128],
                    PS[:, aoff + ci * 128: aoff + (ci + 1) * 128],
                    bkc[:, idx:idx + 1], INV32, ALU.add, ALU.mult)
            else:
                fn = lambda e, ci=ci, idx=idx, aoff=aoff, buf=buf: e.tensor_scalar(
                    vfm[buf][:, idx * 256 + ci * 128: idx * 256 + (ci + 1) * 128],
                    PS[:, aoff + ci * 128: aoff + (ci + 1) * 128],
                    bvc[:, idx:idx + 1], None, ALU.add)
            E("vector", fn, waits=[("t", tv)] if ci == 0 else (),
              incs=[("v", 1)] if ci == 1 else ())
        acc_free[bank] = ("v", now("v"))

    def emit_swap(gi, ci, hh):
        buf = gi % 2
        bank = hh
        aoff = ACC[bank]
        E("tensor", lambda e, hh=hh, ci=ci, aoff=aoff, buf=buf: e.matmul(
            PS[:, aoff:aoff + 512],
            pswp[:, :], qht[buf][ci][:, hh * 512:(hh + 1) * 512],
            start=True, stop=True),
          waits=[("v", qkv_done[gi]), acc_free[bank]], incs=[("t", 1)])
        E("vector", lambda e, hh=hh, ci=ci, aoff=aoff, buf=buf: e.tensor_copy(
            qhts[buf][ci][:, hh * 512:(hh + 1) * 512], PS[:, aoff:aoff + 512]),
          waits=[("t", now("t"))])
        E("vector", lambda e: e.drain(), incs=[("v", 1)])
        acc_free[bank] = ("v", now("v"))
        swap_done[gi] = now("v")

    def fill_bg(gi):
        for ci in range(2):
            for j0 in (0, 4):
                bgq.append(lambda ci=ci, j0=j0: emit_xt_batch(gi, ci, j0))
        def _xfm_mark(gi=gi):
            E("vector", lambda e: e.drain(), incs=[("v", 1)])
            xfm_ready[gi] = now("v")
        bgq.append(_xfm_mark)
        for kind in ("q", "k", "vv"):
            for idx in range(8):
                bgq.append(lambda kind=kind, idx=idx: emit_qkv_round(gi, kind, idx))
        def _qkv_mark(gi=gi):
            E("vector", lambda e: e.drain(), incs=[("v", 1)])
            qkv_done[gi] = now("v")
        bgq.append(_qkv_mark)
        bgq.append(lambda: emit_bound_vec(gi, 0))
        bgq.append(lambda: emit_bound_vec(gi, 1))
        for ci in range(2):
            for hh in range(2):
                bgq.append(lambda ci=ci, hh=hh: emit_swap(gi, ci, hh))
                if ci == 0 and hh == 0:
                    bgq.append(lambda: emit_bound_vec(gi, 2))
                elif ci == 0 and hh == 1:
                    bgq.append(lambda: emit_bound_vec(gi, 3))
        bgq.append(lambda: emit_bound_mm(gi))
        if 1 <= gi <= 11:
            for t in range(4):
                idx = (gi - 1) * 4 + t
                if idx < len(GU_TILES):
                    for part in range(3):
                        def _gu(idx=idx, part=part):
                            prep_gu_tile(*GU_TILES[idx], part)
                            if idx == len(GU_TILES) - 1 and part == 2:
                                prep_dwst_ref[0] = now("dwst")
                        bgq.append(_gu)

    bound_vec_v = {}

    def emit_bound_vec(gi, part):
        buf = gi % 2
        if part == 0:
            E("vector", lambda e, buf=buf: e.tensor_reduce(
                tiny[:, T_MA:T_MA + 1], qht[buf][0][:, :], AX.X, ALU.max,
                apply_absolute_value=True),
              waits=[("v", now("v"))])
            return
        DR("vector")
        if part == 1:
            src = lambda: qht[buf][1][:, :]
            E("vector", lambda e, buf=buf: e.tensor_reduce(
                tiny[:, T_TMP1:T_TMP1 + 1], qht[buf][1][:, :], AX.X, ALU.max,
                apply_absolute_value=True))
        elif part == 2:
            E("vector", lambda e, buf=buf: e.tensor_reduce(
                tiny[:, T_TMP1:T_TMP1 + 1], kfm[buf][:, 0:1024], AX.X, ALU.max,
                apply_absolute_value=True))
        else:
            E("vector", lambda e, buf=buf: e.tensor_reduce(
                tiny[:, T_TMP1:T_TMP1 + 1], kfm[buf][:, 1024:2048], AX.X, ALU.max,
                apply_absolute_value=True))
        DR("vector")
        E("vector", lambda e: e.tensor_tensor(
            tiny[:, T_MA:T_MA + 1], tiny[:, T_MA:T_MA + 1], tiny[:, T_TMP1:T_TMP1 + 1], ALU.max))
        if part == 3:
            E("vector", lambda e: e.drain(), incs=[("v", 1)])
            bound_vec_v[gi] = now("v")

    def emit_bound_mm(gi):
        buf = gi % 2
        E("tensor", lambda e: e.matmul(
            PS[0:1, 3584:3712], tiny[:, T_MA:T_MA + 1], eyeF[:, :], start=True, stop=True),
          waits=[("v", bound_vec_v[gi]), bank7_free_ref[0]], incs=[("t", 1)])
        E("vector", lambda e: e.tensor_reduce(
            tiny[0:1, T_MA:T_MA + 1], PS[0:1, 3584:3712], AX.X, ALU.max),
          waits=[("t", now("t"))])
        DR("vector")
        E("vector", lambda e: e.tensor_tensor(
            tiny[0:1, T_TMP1:T_TMP1 + 1], tiny[0:1, T_MA:T_MA + 1], tiny[0:1, T_MA:T_MA + 1], ALU.mult))
        DR("vector")
        E("vector", lambda e: e.tensor_scalar(
            tiny[0:1, T_NEGC:T_NEGC + 1], tiny[0:1, T_TMP1:T_TMP1 + 1], -64.0, None, ALU.mult))
        E("vector", lambda e: e.memset(stg_f32[0:1, 0:128], 1.0))
        E("vector", lambda e: e.drain(), incs=[("v", 1)])
        E("tensor", lambda e: e.matmul(
            PS[0:128, 3712:3713], stg_f32[0:1, 0:128], tiny[0:1, T_NEGC:T_NEGC + 1],
            start=True, stop=True),
          waits=[("v", now("v"))], incs=[("t", 1)])
        E("vector", lambda e, buf=buf: e.tensor_copy(negc_col[buf][:, :], PS[:, 3712:3713]),
          waits=[("t", now("t"))])
        E("vector", lambda e: e.drain(), incs=[("v", 1)])
        negc_v[gi] = now("v")
        bank7_free_ref[0] = ("v", negc_v[gi])

    bank7_free_ref = [bank7_free]

    load_group(0)
    load_group(1)
    fill_bg(0)
    bg_flush()
    fill_bg(1)

    pend = [None]

    def make_pend(gi, ci, diag_v_c, zadd_v_c):
        buf = gi % 2
        c = 2 * gi + ci
        st = {}

        def p_diag():
            E("tensor", lambda e: e.matmul(
                PS[0:16, 3584:3712], diag_tm[:, :], eyeF[:, :], start=True, stop=True),
              waits=[("v", diag_v_c), bank7_free_ref[0]], incs=[("t", 1)])
            E("vector", lambda e: e.tensor_copy(diagT_sb[:, :], PS[0:16, 3584:3712]),
              waits=[("t", now("t"))])
            E("vector", lambda e: e.drain(), incs=[("v", 1)])
            diagT_v = now("v")
            bank7_free_ref[0] = ("v", diagT_v)
            for fb in range(16):
                ft, par = fb // 2, fb % 2
                boff = par * 64
                E("tensor", lambda e, fb=fb, ft=ft, boff=boff: e.matmul(
                    PS[boff:boff + 64, 1024 + ft * 128: 1024 + (ft + 1) * 128],
                    sel_sb[:, fb * 64:(fb + 1) * 64], diagT_sb[:, :],
                    start=True, stop=True),
                  waits=[("v", diagT_v), ("s", st["lexp1"])] if fb == 0 else (),
                  incs=[("t", 1)] if fb == 15 else ())
            dbc_t = now("t")
            for ft in range(8):
                E("vector", lambda e, ft=ft, ci=ci, buf=buf: e.tensor_tensor(
                    avfm[:, ft * 128:(ft + 1) * 128],
                    vfm[buf][:, ft * 256 + ci * 128: ft * 256 + (ci + 1) * 128],
                    PS[:, 1024 + ft * 128: 1024 + (ft + 1) * 128], ALU.mult),
                  waits=[("t", dbc_t)] if ft == 0 else ())
            E("vector", lambda e: e.drain(), incs=[("v", 1)])
            st["av_v"] = now("v")

        def p_o(fh):
            aoff = ACC[fh]
            for j in range(8):
                E("tensor", lambda e, j=j, fh=fh, aoff=aoff: e.matmul(
                    PS[:, aoff:aoff + 512],
                    avfm[:, j * 128:(j + 1) * 128],
                    wo_sb[:, j * 1024 + fh * 512: j * 1024 + (fh + 1) * 512],
                    start=(j == 0), stop=False),
                  waits=[("v", st["av_v"]), acc_free[fh]] if j == 0 else ())
            E("tensor", lambda e, fh=fh, aoff=aoff: e.matmul(
                PS[:, aoff:aoff + 512],
                ones_r[0:1, :], bo_row[0:1, fh * 512:(fh + 1) * 512],
                start=False, stop=True), incs=[("t", 1)])
            st["o_t%d" % fh] = now("t")

        def p_ln():
            for fh in range(2):
                aoff = ACC[fh]
                E("vector", lambda e, fh=fh, aoff=aoff, buf=buf, ci=ci: e.scalar_tensor_tensor(
                    xres[:, fh * 512:(fh + 1) * 512], PS[:, aoff:aoff + 512],
                    1.0, x_tm[buf][:, ci * 1024 + fh * 512: ci * 1024 + (fh + 1) * 512],
                    ALU.mult, ALU.add,
                    accum_out=tiny[:, T_S1A + fh:T_S1A + fh + 1]),
                  waits=[("t", st["o_t%d" % fh])], incs=[("v", 1)])
                acc_free[fh] = ("v", now("v"))
            DR("vector")
            E("vector", lambda e: e.tensor_tensor(
                tiny[:, T_S1:T_S1 + 1], tiny[:, T_S1A:T_S1A + 1], tiny[:, T_S1B:T_S1B + 1], ALU.add))
            E("vector", lambda e, buf=buf, ci=ci: e.scalar_tensor_tensor(
                x_tm[buf][:, ci * 1024:(ci + 1) * 1024], xres[:, :], 1.0, xres[:, :],
                ALU.mult, ALU.mult, accum_out=tiny[:, T_S2:T_S2 + 1]))
            DR("vector")
            E("vector", lambda e: e.tensor_scalar(
                tiny[:, T_NEGMEAN:T_NEGMEAN + 1], tiny[:, T_S1:T_S1 + 1], -1.0 / D, None, ALU.mult))
            E("vector", lambda e: e.tensor_scalar(
                tiny[:, T_VP:T_VP + 1], tiny[:, T_S2:T_S2 + 1], 1.0 / D, EPS, ALU.mult, ALU.add))
            DR("vector")
            E("vector", lambda e: e.tensor_tensor(
                tiny[:, T_TMP1:T_TMP1 + 1], tiny[:, T_NEGMEAN:T_NEGMEAN + 1],
                tiny[:, T_NEGMEAN:T_NEGMEAN + 1], ALU.mult))
            DR("vector")
            E("vector", lambda e: e.tensor_tensor(
                tiny[:, T_VP:T_VP + 1], tiny[:, T_VP:T_VP + 1], tiny[:, T_TMP1:T_TMP1 + 1],
                ALU.subtract), incs=[("v", 1)])
            E("scalar", lambda e: e.activation(
                tiny[:, T_TMP2:T_TMP2 + 1], tiny[:, T_VP:T_VP + 1], AF.Ln),
              waits=[("v", now("v"))])
            DR("scalar")
            E("scalar", lambda e: e.activation(
                tiny[:, T_RSTD:T_RSTD + 1], tiny[:, T_TMP2:T_TMP2 + 1], AF.Exp, scale=-0.5),
              incs=[("s", 1)])
            rstd_s = now("s")
            xb = c % 2
            E("vector", lambda e, xb=xb: e.tensor_scalar(
                xln_buf[xb][:, :], xres[:, :],
                tiny[:, T_NEGMEAN:T_NEGMEAN + 1], tiny[:, T_RSTD:T_RSTD + 1],
                ALU.add, ALU.mult),
              waits=[("s", rstd_s), xln_free[xb]])
            DR("vector")
            E("vector", lambda e, xb=xb: e.tensor_reduce(
                tiny[:, T_TMP1:T_TMP1 + 1], xln_buf[xb][:, :], AX.X, ALU.max,
                apply_absolute_value=True))
            DR("vector")
            E("vector", lambda e: e.tensor_tensor(
                tiny[:, T_GMAXRUN:T_GMAXRUN + 1], tiny[:, T_GMAXRUN:T_GMAXRUN + 1],
                tiny[:, T_TMP1:T_TMP1 + 1], ALU.max), incs=[("v", 1)])
            E("sync", lambda e, xb=xb, c=c: e.dma_start(
                out=xln1_h[c, :, :], in_=xln_buf[xb][:, :]),
              waits=[("v", now("v"))], incs=[("dst1", 16)])
            xln_free[xb] = ("dst1", now("dst1"))
            st["done_v"] = now("v")

        return {"diag": p_diag, "o0": lambda: p_o(0), "o1": lambda: p_o(1),
                "ln": p_ln, "st": st}

    def drain_pend():
        if pend[0] is not None:
            p = pend[0]
            p["diag"](); p["o0"](); p["o1"](); p["ln"]()
            pend[0] = None
            return p["st"]
        return None

    for gi in range(32):
        buf = gi % 2
        for ci in range(2):
            c = 2 * gi + ci

            swap_v = swap_done[gi]

            chunk_pops = [0]
            if pend[0] is not None:
                pend[0]["diag"]()
                sev_free = ("v", pend[0]["st"]["av_v"])
            for fb in range(16):
                ft, par = fb // 2, fb % 2
                boff = par * 64
                for half in range(2):
                    soff = half * 1024
                    if half == 0:
                        srcb = qht[buf][ci] if par == 0 else qhts[buf][ci]
                        uses_swap = par == 1
                    else:
                        srcb = qhts[buf][ci] if par == 0 else qht[buf][ci]
                        uses_swap = par == 0
                    w = [("s", last_exp[half])]
                    if uses_swap:
                        w.append(("v", swap_v))
                    elif fb <= 1:
                        w.append(("v", qkv_done[gi]))
                    if half == 0 and fb == 0:
                        w.append(sev_free)
                    for nn in range(2):
                        E("tensor", lambda e, ft=ft, boff=boff, soff=soff, srcb=srcb, nn=nn, ci=ci, buf=buf: e.matmul(
                            PS[:, soff + nn * 512: soff + (nn + 1) * 512],
                            kfm[buf][boff:boff + 64, ft * 256 + ci * 128: ft * 256 + (ci + 1) * 128],
                            srcb[boff:boff + 64, nn * 512:(nn + 1) * 512],
                            start=True, stop=True),
                          waits=w if nn == 0 else (), incs=[("t", 1)] if nn == 1 else ())
                    tv = now("t")
                    zb = (c % 2) * 80
                    zcol = zb + 16 * half + fb
                    w2 = [("t", tv), ("v", last_diag[half]), ("v", negc_v[gi])]
                    E("scalar", lambda e, half=half, soff=soff, zcol=zcol, buf=buf: e.activation(
                        e_sb[half][:, :], PS[:, soff:soff + 1024], AF.Exp,
                        bias=negc_col[buf][:, :], scale=1.0,
                        accum_out=zzt[:, zcol:zcol + 1]),
                      waits=w2, incs=[("s", 1)])
                    last_exp[half] = now("s")
                    if half == par:
                        E("scalar", lambda e: e.drain(), incs=[("sd", 1)])
                        E("vector", lambda e, ft=ft, fb=fb, half=half, zb=zb: e.scalar_tensor_tensor(
                            junkb[:, :], e_sb[half][:, ft * 128:(ft + 1) * 128],
                            1.0, eyeB[:, :], ALU.mult, ALU.mult,
                            accum_out=zzt[:, zb + 32 + fb:zb + 32 + fb + 1]),
                          waits=[("sd", now("sd"))], incs=[("v", 1)])
                        last_diag[half] = now("v")
                    slot = fb * 2 + half
                    if slot == 0 and pend[0] is not None:
                        pend[0]["o0"]()
                    elif slot == 1 and pend[0] is not None:
                        pend[0]["o1"]()
                    elif slot == 2 and pend[0] is not None:
                        pend[0]["ln"]()
                        pend[0] = None
                    elif slot >= 3 and chunk_pops[0] < 29:
                        bg_pop(1)
                        chunk_pops[0] += 1

            zb = (c % 2) * 80
            E("scalar", lambda e: e.drain(), waits=[("s", last_exp[1])], incs=[("s", 1)])
            E("vector", lambda e, zb=zb: e.tensor_tensor(
                zzt[:, zb + 48:zb + 64], zzt[:, zb + 0:zb + 16], zzt[:, zb + 16:zb + 32], ALU.add),
              waits=[("s", now("s"))], incs=[("v", 1)])
            zadd_v = now("v")
            DR("vector")
            E("vector", lambda e, zb=zb: e.tensor_scalar(
                zzt[:, zb + 48:zb + 64], zzt[:, zb + 48:zb + 64], 1e-30, None, ALU.max))
            DR("vector")
            E("vector", lambda e, zb=zb: e.reciprocal(zzt[:, zb + 64:zb + 80], zzt[:, zb + 48:zb + 64]))
            DR("vector")
            E("vector", lambda e, zb=zb: e.tensor_tensor(
                diag_tm[:, :], zzt[:, zb + 32:zb + 48], zzt[:, zb + 64:zb + 80], ALU.mult))
            E("vector", lambda e: e.drain(), incs=[("v", 1)])
            p = make_pend(gi, ci, now("v"), zadd_v)
            p["st"]["lexp1"] = last_exp[1]
            pend[0] = p

            if ci == 0 and gi >= 1 and gi + 1 <= 31:
                xtm_free[(gi - 1) % 2] = ("v", now("v"))
                load_group(gi + 1)
                fill_bg(gi + 1)

            if debug and gi == 0 and ci == 0:
                drain_pend()
                vv = now("v")
                for nm, src_t in (("d_xfm", xfm[0]), ("d_qht", qht[0][0]), ("d_qhts", qhts[0][0]),
                                  ("d_kfm", kfm[0]), ("d_vfm", vfm[0]),
                                  ("d_negc", negc_col[0]), ("d_zzt", zzt), ("d_diag", diag_tm),
                                  ("d_av", avfm), ("d_xres", xres), ("d_xln", xln_buf[0])):
                    E("sync", lambda e, nm=nm, src_t=src_t: e.dma_start(
                        out=dbg[nm][:, :], in_=src_t[:, :]),
                      waits=[("v", vv)], incs=[("dmsc", 16)])
                for eng in ("vector", "scalar", "tensor", "sync"):
                    sch.wait(eng, ("dmsc", now("dmsc")))
        bg_flush()

    stf = drain_pend()
    m67_free = bank7_free_ref[0]

    E("tensor", lambda e: e.matmul(
        PS[0:1, 3584:3712], tiny[:, T_GMAXRUN:T_GMAXRUN + 1], eyeF[:, :],
        start=True, stop=True),
      waits=[("v", now("v")), m67_free], incs=[("t", 1)])
    E("vector", lambda e: e.tensor_reduce(
        tiny[0:1, T_G11:T_G11 + 1], PS[0:1, 3584:3712], AX.X, ALU.max),
      waits=[("t", now("t"))], incs=[("v", 1)])
    E("gpsimd", lambda e: e.dma_start(out=cc1i_h[:], in_=tiny[0:1, T_G11:T_G11 + 1]),
      waits=[("v", now("v"))], incs=[("g", 16)])
    E("gpsimd", lambda e: e.collective_compute(
        "AllReduce", ALU.max, replica_groups=[list(range(NCORES))],
        ins=[cc1i_h.opt()], outs=[cc1o_h.opt()]),
      waits=[("g", now("g"))], incs=[("cc", 1)])
    E("gpsimd", lambda e: e.dma_start(out=tiny[0:1, T_G11:T_G11 + 1], in_=cc1o_h[:]),
      waits=[("cc", now("cc"))], incs=[("g", 16)])
    gmax_g = now("g")
    E("vector", lambda e: e.tensor_tensor(
        tiny[0:1, T_TMP1:T_TMP1 + 1], tiny[0:1, T_G11:T_G11 + 1],
        tiny[0:1, T_SIGX:T_SIGX + 1], ALU.mult),
      waits=[("g", gmax_g), ("s", sig_done)])
    DR("vector")
    E("vector", lambda e: e.tensor_scalar(
        tiny[0:1, T_SX:T_SX + 1], tiny[0:1, T_TMP1:T_TMP1 + 1], 1.0 / QMAX, None, ALU.mult))
    DR("vector")
    E("vector", lambda e: e.reciprocal(
        tiny[0:1, T_RX:T_RX + 1], tiny[0:1, T_SX:T_SX + 1]))
    E("vector", lambda e: e.memset(stg_f32[0:1, 0:128], 1.0), incs=[("v", 1)])
    E("tensor", lambda e: e.matmul(
        PS[0:128, 3584:3585], stg_f32[0:1, 0:128], tiny[0:1, T_RX:T_RX + 1],
        start=True, stop=True), waits=[("v", now("v"))])
    E("tensor", lambda e: e.matmul(
        PS[0:128, 3585:3586], stg_f32[0:1, 0:128], tiny[0:1, T_SX:T_SX + 1],
        start=True, stop=True), incs=[("t", 1)])
    E("vector", lambda e: e.tensor_copy(rx_col[:, :], PS[:, 3584:3585]),
      waits=[("t", now("t"))])
    E("vector", lambda e: e.tensor_copy(sx_col[:, :], PS[:, 3585:3586]))
    DR("vector")
    E("vector", lambda e: e.tensor_scalar(
        sg_eff[:, :], sgi[:, :], sx_col[:, 0:1], None, ALU.mult))
    E("vector", lambda e: e.tensor_scalar(
        su_eff[:, :], sui[:, :], sx_col[:, 0:1], None, ALU.mult), incs=[("v", 1)])
    bar1_v = now("v")
    E("scalar", lambda e: _load_act_set(e, AF.Silu), waits=[("v", bar1_v)], incs=[("s", 1)])
    silu_loaded = now("s")

    xln_stored = now("dst1")
    hp_free = {k: ("dhp", 0) for k in range(4)}
    xqT_free = {0: ("t", 0), 1: ("t", 0)}
    gu_free = {k: ("v", 0) for k in range(4)}
    wgbuf_free = {0: ("t", 0), 1: ("t", 0)}
    xl_free = {0: ("v", 0), 1: ("v", 0)}
    sil_free = {0: ("v", 0), 1: ("v", 0)}
    xq_evac = {}
    GU_OFF = {0: 0, 1: 512, 2: 1024, 3: 1536}

    def emit_xq(tb, tbuf):
        for k in range(8):
            c = tb * 8 + k
            lb = c % 2
            E("scalar", lambda e, c=c, lb=lb: e.dma_start(
                out=xl_b[lb][:, :], in_=xln1_h[c, :, :]),
              waits=[("dst1", xln_stored), xl_free[lb]], incs=[("dlx", 16)])
            dl = now("dlx")
            E("vector", lambda e, lb=lb: e.tensor_scalar(
                junkf[:, :], xl_b[lb][:, :], rx_col[:, 0:1], None, ALU.mult),
              waits=[("dlx", dl), ("v", bar1_v)], incs=[("v", 1)])
            xl_free[lb] = ("v", now("v"))
            DR("vector")
            E("vector", lambda e: e.tensor_scalar(
                junkf[:, :], junkf[:, :], RC, RC, ALU.add, ALU.subtract))
            DR("vector")
            E("vector", lambda e: e.tensor_scalar(
                xq_tm[:, :], junkf[:, :], QMAX, -QMAX - 1.0, ALU.min, ALU.max),
              incs=[("v", 1)])
            vq = now("v")
            for b in (0, 4):
                for hb in range(b, b + 4):
                    slot = hb % 4
                    poff = 3072 + (slot % 2) * 512 + (slot // 2) * 128
                    w = ()
                    if hb == b:
                        w = [("v", vq), ("v", xq_evac.get(0, 0))]
                        if k == 0 and hb == 0:
                            w.append(xqT_free[tbuf])
                    E("tensor", lambda e, hb=hb, poff=poff: e.transpose(
                        PSB[:, 2 * poff:2 * poff + 128], xq_tm[:, hb * 128:(hb + 1) * 128], eyeB[:, :]),
                      waits=w, incs=[("t", 1)] if hb == b + 3 else ())
                tv = now("t")
                for hb in range(b, b + 4):
                    slot = hb % 4
                    poff = 3072 + (slot % 2) * 512 + (slot // 2) * 128
                    E("vector", lambda e, hb=hb, k=k, poff=poff, tbuf=tbuf: e.tensor_copy(
                        xqT[tbuf][:, hb * 1024 + k * 128: hb * 1024 + (k + 1) * 128],
                        PSB[:, 2 * poff:2 * poff + 128]),
                      waits=[("t", tv)] if hb == b else (), incs=[("v", 1)] if hb == b + 3 else ())
                xq_evac[0] = now("v")
        return now("v")

    if debug:
        E("sync", lambda e: e.dma_start(out=dbg["d_gmax"][:, :], in_=tiny[0:1, T_G11:T_G11 + 1]),
          waits=[("v", bar1_v)], incs=[("dmsc", 16)])
        E("sync", lambda e: e.dma_start(out=dbg["d_rx"][:, :], in_=rx_col[:, :]),
          waits=[("v", bar1_v)], incs=[("dmsc", 16)])
        E("sync", lambda e: e.dma_start(out=dbg["d_sgi"][:, :], in_=sgi[:, :]),
          waits=[("v", bar1_v)], incs=[("dmsc", 16)])
        for eng in ("vector", "scalar", "tensor", "sync"):
            sch.wait(eng, ("dmsc", now("dmsc")))
    xq_ready = {0: emit_xq(0, 0)}
    if debug:
        E("sync", lambda e: e.dma_start(out=dbg["d_xq"][:, :],
                                        in_=xq_tm[:, :]),
          waits=[("v", xq_ready[0])], incs=[("dmsc", 16)])
        for eng in ("vector", "scalar", "tensor", "sync"):
            sch.wait(eng, ("dmsc", now("dmsc")))
    hp_seq = 0
    for tb in range(8):
        tbuf = tb % 2
        xqv = xq_ready[tb]
        for j in range(NJ):
            if j == 3 and tb + 1 < 8:
                xq_ready[tb + 1] = emit_xq(tb + 1, (tb + 1) % 2)
            wb = j % 2
            E("sync", lambda e, j=j, wb=wb: e.dma_start(
                out=wgT_sb[wb][:, :], in_=wgT_h[j, :, :]),
              waits=[("dwst", prep_dwst_ref[0]), wgbuf_free[wb]], incs=[("dwg", 16)])
            E("sync", lambda e, j=j, wb=wb: e.dma_start(
                out=wuT_sb[wb][:, :], in_=wuT_h[j, :, :]), incs=[("dwg", 16)])
            dwgn = now("dwg")
            for th in range(2):
                goff = GU_OFF[2 * th]
                uoff = GU_OFF[2 * th + 1]
                for hb in range(8):
                    E("tensor", lambda e, hb=hb, th=th, wb=wb, tbuf=tbuf, goff=goff: e.matmul(
                        PS[:, goff:goff + 512],
                        wgT_sb[wb][:, hb * 128:(hb + 1) * 128],
                        xqT[tbuf][:, hb * 1024 + th * 512: hb * 1024 + (th + 1) * 512],
                        start=(hb == 0), stop=(hb == 7)),
                      waits=[("dwg", dwgn), ("v", xqv), gu_free[2 * th]] if hb == 0 else (),
                      incs=[("t", 1)] if hb == 7 else ())
                gt = now("t")
                for hb in range(8):
                    E("tensor", lambda e, hb=hb, th=th, wb=wb, tbuf=tbuf, uoff=uoff: e.matmul(
                        PS[:, uoff:uoff + 512],
                        wuT_sb[wb][:, hb * 128:(hb + 1) * 128],
                        xqT[tbuf][:, hb * 1024 + th * 512: hb * 1024 + (th + 1) * 512],
                        start=(hb == 0), stop=(hb == 7)),
                      waits=[gu_free[2 * th + 1]] if hb == 0 else (),
                      incs=[("t", 1)] if hb == 7 else ())
                ut = now("t")
                sbi = th
                E("scalar", lambda e, sbi=sbi, j=j, goff=goff: e.activation(
                    sil_sb[sbi][:, :], PS[:, goff:goff + 512], AF.Silu,
                    bias=0.0, scale=sg_eff[:, j:j + 1]),
                  waits=[("t", gt), ("s", silu_loaded), sil_free[sbi]], incs=[("s", 1)])
                sl_s = now("s")
                gu_free[2 * th] = ("s", sl_s)
                hbi = hp_seq % 4
                hp_seq += 1
                E("vector", lambda e, hbi=hbi, sbi=sbi, j=j, uoff=uoff: e.scalar_tensor_tensor(
                    hp_st[hbi][:, :], PS[:, uoff:uoff + 512],
                    su_eff[:, j:j + 1], sil_sb[sbi][:, :], ALU.mult, ALU.mult),
                  waits=[("t", ut), ("s", sl_s), hp_free[hbi]], incs=[("v", 1)])
                gu_free[2 * th + 1] = ("v", now("v"))
                sil_free[sbi] = ("v", now("v"))
                DR("vector")
                E("vector", lambda e, hbi=hbi: e.tensor_reduce(
                    tiny[:, T_TMP1:T_TMP1 + 1], hp_st[hbi][:, :], AX.X, ALU.max,
                    apply_absolute_value=True))
                DR("vector")
                E("vector", lambda e: e.tensor_tensor(
                    tiny[:, T_HMAXRUN:T_HMAXRUN + 1], tiny[:, T_HMAXRUN:T_HMAXRUN + 1],
                    tiny[:, T_TMP1:T_TMP1 + 1], ALU.max), incs=[("v", 1)])
                E("gpsimd", lambda e, hbi=hbi, tb=tb, th=th, j=j: e.dma_start(
                    out=hp_h[2 * tb + th, j, :, :], in_=hp_st[hbi][:, :]),
                  waits=[("v", now("v"))], incs=[("dhp", 16)])
                hp_free[hbi] = ("dhp", now("dhp"))
                if debug and tb == 0 and th == 0 and j == 0:
                    E("sync", lambda e, hbi=hbi: e.dma_start(
                        out=dbg["d_hp"][:, :], in_=hp_st[hbi][:, :]),
                      waits=[("v", now("v"))], incs=[("dmsc", 16)])
                    for eng in ("vector", "scalar", "tensor", "sync"):
                        sch.wait(eng, ("dmsc", now("dmsc")))
            wgbuf_free[wb] = ("t", now("t"))
        xqT_free[tbuf] = ("t", now("t"))

    E("tensor", lambda e: e.matmul(
        PS[0:1, 3584:3712], tiny[:, T_HMAXRUN:T_HMAXRUN + 1], eyeF[:, :],
        start=True, stop=True), waits=[("v", now("v"))], incs=[("t", 1)])
    E("vector", lambda e: e.tensor_reduce(
        tiny[0:1, T_G11:T_G11 + 1], PS[0:1, 3584:3712], AX.X, ALU.max),
      waits=[("t", now("t"))], incs=[("v", 1)])
    E("gpsimd", lambda e: e.dma_start(out=cc2i_h[:], in_=tiny[0:1, T_G11:T_G11 + 1]),
      waits=[("v", now("v"))], incs=[("g", 16)])
    E("gpsimd", lambda e: e.collective_compute(
        "AllReduce", ALU.max, replica_groups=[list(range(NCORES))],
        ins=[cc2i_h.opt()], outs=[cc2o_h.opt()]),
      waits=[("g", now("g"))], incs=[("cc", 1)])
    E("gpsimd", lambda e: e.dma_start(out=tiny[0:1, T_G11:T_G11 + 1], in_=cc2o_h[:]),
      waits=[("cc", now("cc"))], incs=[("g", 16)])
    g2_g = now("g")
    E("vector", lambda e: e.tensor_tensor(
        tiny[0:1, T_TMP1:T_TMP1 + 1], tiny[0:1, T_G11:T_G11 + 1],
        tiny[0:1, T_SIGD:T_SIGD + 1], ALU.mult), waits=[("g", g2_g)])
    DR("vector")
    E("vector", lambda e: e.tensor_scalar(
        tiny[0:1, T_SD:T_SD + 1], tiny[0:1, T_TMP1:T_TMP1 + 1], 1.0 / QMAX, None, ALU.mult))
    DR("vector")
    E("vector", lambda e: e.reciprocal(
        tiny[0:1, T_RX2:T_RX2 + 1], tiny[0:1, T_SD:T_SD + 1]))
    E("vector", lambda e: e.memset(stg_f32[0:1, 0:128], 1.0), incs=[("v", 1)])
    E("tensor", lambda e: e.matmul(
        PS[0:128, 3584:3585], stg_f32[0:1, 0:128], tiny[0:1, T_RX2:T_RX2 + 1],
        start=True, stop=True), waits=[("v", now("v"))])
    E("tensor", lambda e: e.matmul(
        PS[0:128, 3585:3586], stg_f32[0:1, 0:128], tiny[0:1, T_SD:T_SD + 1],
        start=True, stop=True), incs=[("t", 1)])
    E("vector", lambda e: e.tensor_copy(rd_col[:, :], PS[:, 3584:3585]),
      waits=[("t", now("t"))])
    E("vector", lambda e: e.tensor_copy(sd_col[:, :], PS[:, 3585:3586]), incs=[("v", 1)])
    bar2_v = now("v")
    E("scalar", lambda e: _load_act_set(e, AF.Exp), incs=[("s", 1)])

    hp_stored = now("dhp")
    DOWNP = {0: (0, 512), 1: (1024, 1536)}
    down_free = {0: ("v", 0), 1: ("v", 0)}
    hpin_free = {0: ("v", 0), 1: ("v", 0)}
    xlb_free = {0: ("v", bar2_v), 1: ("v", bar2_v)}
    ybuf_free = {0: ("dout", 0), 1: ("dout", 0)}

    for j in range(NJ):
        E("sync", lambda e, j=j: e.dma_start(
            out=wdT_sb[:, j * 1024:(j + 1) * 1024], in_=wdT_h[j, :, :]),
          waits=[("dwst", prep_dwst_ref[0]), ("v", bar1_v)] if j == 0 else (), incs=[("dwg", 16)])
    wdT_g = now("dwg")

    hq_v = {}
    hq_jv = {}
    hqfree = {0: ("t", 0), 1: ("t", 0)}
    hqbufs = {0: hqT, 1: hqT2}
    qscr_free = {0: ("v", bar2_v), 1: ("v", bar2_v)}

    def emit_quant(tg):
        hqb = hqbufs[tg % 2]
        for j in range(NJ):
            hb = j % 2
            for hh in range(2):
                w = []
                if hh == 0:
                    if tg == 0 and j < 2:
                        w.append(("dhp", hp_stored))
                    w.append(hpin_free[hb])
                E("sync", lambda e, tg=tg, j=j, hh=hh, hb=hb: e.dma_start(
                    out=hp_in[hb][:, hh * 512:(hh + 1) * 512],
                    in_=hp_h[2 * tg + hh, j, :, :]),
                  waits=w, incs=[("dhl", 16)])
            dhn = now("dhl")
            qb = j % 2
            E("vector", lambda e, hb=hb, qb=qb: e.tensor_scalar(
                q_scr[qb][:, :], hp_in[hb][:, :], rd_col[:, 0:1], None, ALU.mult),
              waits=[("dhl", dhn), ("v", bar2_v)], incs=[("v", 1)])
            hpin_free[hb] = ("v", now("v"))
            DR("vector")
            E("vector", lambda e, qb=qb: e.tensor_scalar(
                q_scr[qb][:, :], q_scr[qb][:, :], RC, RC, ALU.add, ALU.subtract))
            DR("vector")
            E("vector", lambda e, j=j, qb=qb, hqb=hqb: e.tensor_scalar(
                hqb[:, j * 1024:(j + 1) * 1024], q_scr[qb][:, :], QMAX, -QMAX - 1.0,
                ALU.min, ALU.max),
              waits=[hqfree[tg % 2]] if j == 0 else (), incs=[("v", 1)])
            hq_jv[(tg, j)] = now("v")
        hq_v[tg] = now("v")

    emit_quant(0)
    for tg in range(8):
        hqb = hqbufs[tg % 2]
        for k in range(8):
            c = tg * 8 + k
            seq, pos = c // 16, (c % 16) * 128
            lb = k % 2
            E("scalar", lambda e, c=c, lb=lb: e.dma_start(
                out=xl_b[lb][:, :], in_=xln1_h[c, :, :]),
              waits=[xlb_free[lb]], incs=[("dlx", 16)])
            dxl = now("dlx")
            pb = k % 2
            o0, o1 = DOWNP[pb]
            for j in range(NJ):
                E("tensor", lambda e, j=j, k=k, o0=o0, hqb=hqb: e.matmul(
                    PS[:, o0:o0 + 512], hqb[:, j * 1024 + k * 128: j * 1024 + (k + 1) * 128],
                    wdT_sb[:, j * 1024: j * 1024 + 512],
                    start=(j == 0), stop=(j == NJ - 1)),
                  waits=(([("v", hq_jv[(tg, j)] if k == 0 else hq_v[tg]),
                           ("dwg", wdT_g), down_free[pb]] if j == 0 else
                          ([("v", hq_jv[(tg, j)])] if k == 0 else ()))))
                E("tensor", lambda e, j=j, k=k, o1=o1, hqb=hqb: e.matmul(
                    PS[:, o1:o1 + 512], hqb[:, j * 1024 + k * 128: j * 1024 + (k + 1) * 128],
                    wdT_sb[:, j * 1024 + 512: (j + 1) * 1024],
                    start=(j == 0), stop=(j == NJ - 1)),
                  incs=[("t", 1)] if j == NJ - 1 else ())
            dt = now("t")
            if k == 0 and tg + 1 < 8:
                emit_quant(tg + 1)
            for fh in range(2):
                off = o0 if fh == 0 else o1
                E("vector", lambda e, fh=fh, off=off, lb=lb: e.scalar_tensor_tensor(
                    xres[:, fh * 512:(fh + 1) * 512], PS[:, off:off + 512],
                    sd_col[:, 0:1], xl_b[lb][:, fh * 512:(fh + 1) * 512],
                    ALU.mult, ALU.add,
                    accum_out=tiny[:, T_S1A + fh:T_S1A + fh + 1]),
                  waits=[("t", dt), ("dlx", dxl)] if fh == 0 else (),
                  incs=[("v", 1)] if fh == 1 else ())
            down_free[pb] = ("v", now("v"))
            xlb_free[lb] = ("v", now("v"))
            DR("vector")
            E("vector", lambda e: e.tensor_tensor(
                tiny[:, T_S1:T_S1 + 1], tiny[:, T_S1A:T_S1A + 1],
                tiny[:, T_S1B:T_S1B + 1], ALU.add))
            E("vector", lambda e: e.scalar_tensor_tensor(
                xq_tm[:, :], xres[:, :], 1.0, xres[:, :], ALU.mult, ALU.mult,
                accum_out=tiny[:, T_S2:T_S2 + 1]))
            DR("vector")
            E("vector", lambda e: e.tensor_scalar(
                tiny[:, T_NEGMEAN:T_NEGMEAN + 1], tiny[:, T_S1:T_S1 + 1], -1.0 / D, None, ALU.mult))
            E("vector", lambda e: e.tensor_scalar(
                tiny[:, T_VP:T_VP + 1], tiny[:, T_S2:T_S2 + 1], 1.0 / D, EPS, ALU.mult, ALU.add))
            DR("vector")
            E("vector", lambda e: e.tensor_tensor(
                tiny[:, T_TMP1:T_TMP1 + 1], tiny[:, T_NEGMEAN:T_NEGMEAN + 1],
                tiny[:, T_NEGMEAN:T_NEGMEAN + 1], ALU.mult))
            DR("vector")
            E("vector", lambda e: e.tensor_tensor(
                tiny[:, T_VP:T_VP + 1], tiny[:, T_VP:T_VP + 1],
                tiny[:, T_TMP1:T_TMP1 + 1], ALU.subtract), incs=[("v", 1)])
            E("scalar", lambda e: e.activation(
                tiny[:, T_TMP2:T_TMP2 + 1], tiny[:, T_VP:T_VP + 1], AF.Ln),
              waits=[("v", now("v"))])
            DR("scalar")
            E("scalar", lambda e: e.activation(
                tiny[:, T_RSTD:T_RSTD + 1], tiny[:, T_TMP2:T_TMP2 + 1], AF.Exp, scale=-0.5),
              incs=[("s", 1)])
            yb = k % 2
            E("vector", lambda e, yb=yb: e.tensor_scalar(
                y_buf[yb][:, :], xres[:, :],
                tiny[:, T_NEGMEAN:T_NEGMEAN + 1], tiny[:, T_RSTD:T_RSTD + 1],
                ALU.add, ALU.mult),
              waits=[("s", now("s")), ybuf_free[yb]], incs=[("v", 1)])
            E("gpsimd", lambda e, yb=yb, seq=seq, pos=pos: e.dma_start(
                out=y_h[seq, pos:pos + 128, :], in_=y_buf[yb][:, :]),
              waits=[("v", now("v"))], incs=[("dout", 16)])
            ybuf_free[yb] = ("dout", now("dout"))
            if debug and tg == 0 and k == 0:
                E("sync", lambda e: e.dma_start(out=dbg["d_xres2"][:, :], in_=xres[:, :]),
                  waits=[("v", now("v"))], incs=[("dmsc", 16)])
                E("sync", lambda e, yb=yb: e.dma_start(out=dbg["d_y"][:, :], in_=y_buf[yb][:, :]),
                  waits=[("v", now("v"))], incs=[("dmsc", 16)])
                for eng in ("vector", "scalar", "tensor", "sync"):
                    sch.wait(eng, ("dmsc", now("dmsc")))
        hqfree[tg % 2] = ("t", now("t"))
    sch.wait("sync", ("dout", now("dout")))
    sch.wait("gpsimd", ("dout", now("dout")))

    block = es.enter_context(nc.Block())
    sch.replay(block)
    return nc


def kernel(**inputs):
    X = np.ascontiguousarray(np.asarray(inputs["L0_attn_X_embed"], dtype=np.float32))
    nc = build_program()
    full = {k2: np.ascontiguousarray(np.asarray(inputs[k1], dtype=np.float32))
            for k1, k2 in (
                ("L0_attn_W_q", "wq"), ("L0_attn_b_q", "bq"),
                ("L0_attn_W_k", "wk"), ("L0_attn_b_k", "bk"),
                ("L0_attn_W_v", "wv"), ("L0_attn_b_v", "bv"),
                ("L0_attn_W_o", "wo"), ("L0_attn_b_o", "bo"),
                ("L0_ffn_clip_x", "cx"), ("L0_ffn_clip_down", "cd"),
                ("L0_ffn_w_gate", "wg"), ("L0_ffn_w_up", "wu"),
                ("L0_ffn_w_down", "wd"))}
    in_maps = []
    for c in range(NCORES):
        m = dict(full)
        m["x"] = np.ascontiguousarray(X[SEQ * c:SEQ * (c + 1)])
        in_maps.append(m)
    res = run_bass_kernel_spmd(nc, in_maps, core_ids=list(range(NCORES)))
    return np.concatenate([r["y"] for r in res.results], axis=0)




# version 43
# speedup vs baseline: 1.0216x; 1.0216x over previous
import numpy as np
import ml_dtypes
from contextlib import ExitStack

import concourse.bass as bass
import concourse.mybir as mybir
from concourse.bass_utils import run_bass_kernel_spmd

F32 = mybir.dt.float32
BF16 = mybir.dt.bfloat16
F16 = mybir.dt.float16
AF = mybir.ActivationFunctionType
ALU = mybir.AluOpType
AX = mybir.AxisListType
BFNP = ml_dtypes.bfloat16

RC = 12582912.0
QMAX = 127.0
EPS = 1e-5
INV32 = 1.0 / 32.0

SEQ = 4
L = 2048
D = 1024
NCHUNK = 64
II = 2816
NJ = 22
NCORES = 8

SET_SIGMOID = 2
SET_LNEXP = 6
SET_SILU = 18


def _load_act_set(scalar, func):
    try:
        from concourse.hw_specs import get_activation_tables
        tabs = get_activation_tables(scalar.bass.m.arch)
        want = {AF.Sigmoid: "sigmoid_and_others", AF.Exp: "natural_log_exp_and_others",
                AF.Ln: "natural_log_exp_and_others", AF.Silu: "silu_and_others"}[func]
        idx = list(tabs.keys()).index(want)
    except Exception:
        idx = {AF.Sigmoid: SET_SIGMOID, AF.Exp: SET_LNEXP,
               AF.Ln: SET_LNEXP, AF.Silu: SET_SILU}[func]
    return scalar.add_instruction(
        mybir.InstLoadActFuncSet(
            name=scalar.bass.get_next_instruction_name(),
            act_func_set_id=idx, ins=[], outs=[]))


class Sch:
    ENGINES = ("sync", "tensor", "vector", "scalar", "gpsimd")

    def __init__(self, nc, es):
        self.nc = nc
        self.es = es
        self.q = {e: [] for e in self.ENGINES}
        self.cnt = {}
        self.sems = {}

    def sem(self, name):
        s = self.es.enter_context(self.nc.semaphore(name))
        self.sems[name] = s
        self.cnt[name] = 0
        return s

    def emit(self, eng, fn, waits=(), incs=()):
        iv = []
        for s, n in incs:
            self.cnt[s] += n
            iv.append((s, n))
        self.q[eng].append((fn, tuple(waits), tuple(iv)))

    def wait(self, eng, *waits):
        self.q[eng].append((None, tuple(waits), ()))

    def now(self, s):
        return self.cnt[s]

    def replay(self, block):
        def mk(engname):
            def body(engine):
                for fn, waits, incs in self.q[engname]:
                    for s, v in waits:
                        if v > 0:
                            engine.wait_ge(self.sems[s], v)
                    if fn is None:
                        continue
                    inst = fn(engine)
                    for s, n in incs:
                        inst = inst.then_inc(self.sems[s], n)
            return body
        block.sync(mk("sync"))
        block.tensor(mk("tensor"))
        block.vector(mk("vector"))
        block.scalar(mk("scalar"))
        block.gpsimd(mk("gpsimd"))


def build_program(debug=False):
    nc = bass.Bass("TRN2", target_bir_lowering=False)
    es = ExitStack()
    nc._keep_es = es

    x_h = nc.dram_tensor("x", [SEQ, L, D], F32, kind="ExternalInput").ap()
    wq_h = nc.dram_tensor("wq", [D, D], F32, kind="ExternalInput").ap()
    bq_h = nc.dram_tensor("bq", [D], F32, kind="ExternalInput").ap()
    wk_h = nc.dram_tensor("wk", [D, D], F32, kind="ExternalInput").ap()
    bk_h = nc.dram_tensor("bk", [D], F32, kind="ExternalInput").ap()
    wv_h = nc.dram_tensor("wv", [D, D], F32, kind="ExternalInput").ap()
    bv_h = nc.dram_tensor("bv", [D], F32, kind="ExternalInput").ap()
    wo_h = nc.dram_tensor("wo", [D, D], F32, kind="ExternalInput").ap()
    bo_h = nc.dram_tensor("bo", [D], F32, kind="ExternalInput").ap()
    cx_h = nc.dram_tensor("cx", [1], F32, kind="ExternalInput").ap()
    cd_h = nc.dram_tensor("cd", [1], F32, kind="ExternalInput").ap()
    wg_h = nc.dram_tensor("wg", [II, D], F32, kind="ExternalInput").ap()
    wu_h = nc.dram_tensor("wu", [II, D], F32, kind="ExternalInput").ap()
    wd_h = nc.dram_tensor("wd", [D, II], F32, kind="ExternalInput").ap()
    y_h = nc.dram_tensor("y", [SEQ, L, D], F32, kind="ExternalOutput").ap()

    dbg = {}
    if debug:
        for nm, shp, dt in (("d_xfm", [128, 2048], BF16), ("d_qht", [128, 1024], BF16),
                            ("d_qhts", [128, 1024], BF16),
                            ("d_kfm", [128, 2048], BF16), ("d_vfm", [128, 2048], BF16),
                            ("d_negc", [128, 1], F32), ("d_zzt", [128, 160], F32),
                            ("d_diag", [128, 16], F32), ("d_av", [128, 1024], BF16),
                            ("d_xres", [128, 1024], F32), ("d_xln", [128, 1024], F32),
                            ("d_gmax", [1, 1], F32), ("d_rx", [128, 1], F32),
                            ("d_xq", [128, 1024], BF16), ("d_sgi", [128, 22], F32),
                            ("d_hq", [128, 1024], BF16), ("d_wdt", [128, 1024], BF16),
                            ("d_rd", [128, 1], F32), ("d_sd", [128, 1], F32),
                            ("d_g2", [1, 1], F32), ("d_hp", [128, 512], F16),
                            ("d_y", [128, 1024], F32), ("d_xres2", [128, 1024], F32)):
            dbg[nm] = nc.dram_tensor(nm, shp, dt, kind="ExternalOutput").ap()

    xln1_h = nc.dram_tensor("xln1_s", [NCHUNK, 128, D], F32).ap()
    hp_h = nc.dram_tensor("hp_s", [16, NJ, 128, 512], F16).ap()
    wgT_h = nc.dram_tensor("wgT_s", [NJ, 128, 1024], BF16).ap()
    wuT_h = nc.dram_tensor("wuT_s", [NJ, 128, 1024], BF16).ap()
    wdT_h = nc.dram_tensor("wdT_s", [NJ, 128, 1024], BF16).ap()
    cc1i_h = nc.dram_tensor("cc1_in", [1], F32).ap()
    cc1o_h = nc.dram_tensor("cc1_out", [1], F32).ap()
    cc2i_h = nc.dram_tensor("cc2_in", [1], F32).ap()
    cc2o_h = nc.dram_tensor("cc2_out", [1], F32).ap()

    eye_f = nc.inline_tensor(np.eye(128, dtype=np.float32), "eye_f")
    eye_b = nc.inline_tensor(np.eye(128, dtype=BFNP), "eye_b")
    pswap_np = np.zeros((128, 128), dtype=BFNP)
    for p in range(128):
        pswap_np[(p + 64) % 128, p] = 1.0
    pswap = nc.inline_tensor(pswap_np, "pswap")
    sel_np = np.zeros((16, 16 * 64), dtype=BFNP)
    for fb in range(16):
        sel_np[fb, fb * 64:(fb + 1) * 64] = 1.0
    sel_c = nc.inline_tensor(sel_np, "sel_c")

    def sb(name, shape, dt):
        return es.enter_context(nc.sbuf_tensor(name, shape, dt))

    eyeF = sb("eyeF", [128, 128], F32)
    eyeB = sb("eyeB", [128, 128], BF16)
    pswp = sb("pswp", [128, 128], BF16)
    sel_sb = sb("sel_sb", [16, 16 * 64], BF16)
    ones_r = sb("ones_r", [1, 128], BF16)
    bo_row = sb("bo_row", [1, 1024], BF16)
    stg_f32 = sb("stg_f32", [1, 1024], F32)
    bqc = sb("bqc", [128, 8], F32)
    bkc = sb("bkc", [128, 8], F32)
    bvc = sb("bvc", [128, 8], F32)
    sgi = sb("sgi", [128, NJ], F32)
    sui = sb("sui", [128, NJ], F32)
    sg_eff = sb("sg_eff", [128, NJ], F32)
    su_eff = sb("su_eff", [128, NJ], F32)
    tiny = sb("tiny", [128, 32], F32)
    (T_SIGX, T_SIGD, T_G11, T_NEGC, T_MA, T_RX, T_SX, T_SD, T_GMAXRUN, T_HMAXRUN,
     T_S1A, T_S1B, T_S1, T_NEGMEAN, T_S2, T_VP, T_RSTD, T_TMP1, T_TMP2, T_RX2,
     T_RCP, T_RCN, T_GUR) = range(23)
    zzt = sb("zzt", [128, 160], F32)
    diag_tm = sb("diag_tm", [128, 16], F32)
    diagT_sb = sb("diagT_sb", [16, 128], BF16)
    negc_col = [sb("negc_col0", [128, 1], F32), sb("negc_col1", [128, 1], F32)]
    rx_col = sb("rx_col", [128, 1], F32)
    sx_col = sb("sx_col", [128, 1], F32)
    rd_col = sb("rd_col", [128, 1], F32)
    sd_col = sb("sd_col", [128, 1], F32)

    wq_sb = sb("wq_sb", [128, 8 * 1024], BF16)
    wk_sb = sb("wk_sb", [128, 8 * 1024], BF16)
    wv_sb = sb("wv_sb", [128, 8 * 1024], BF16)
    wo_sb = sb("wo_sb", [128, 8 * 1024], BF16)
    x_tm = [sb("x_tm0", [128, 2048], F32), sb("x_tm1", [128, 2048], F32)]
    xfm = [sb("xfm0", [128, 8 * 256], BF16), sb("xfm1", [128, 8 * 256], BF16)]
    qht = [[sb("qht00", [128, 1024], BF16), sb("qht01", [128, 1024], BF16)],
           [sb("qht10", [128, 1024], BF16), sb("qht11", [128, 1024], BF16)]]
    qhts = [[sb("qhts00", [128, 1024], BF16), sb("qhts01", [128, 1024], BF16)],
            [sb("qhts10", [128, 1024], BF16), sb("qhts11", [128, 1024], BF16)]]
    kfm = [sb("kfm0", [128, 8 * 256], BF16), sb("kfm1", [128, 8 * 256], BF16)]
    vfm = [sb("vfm0", [128, 8 * 256], BF16), sb("vfm1", [128, 8 * 256], BF16)]
    e_sb = [sb("e_sb0", [128, 1024], BF16), sb("e_sb1", [128, 1024], BF16)]
    avfm = sb("avfm", [128, 1024], BF16)
    xres = sb("xres", [128, 1024], F32)
    junkb = sb("junkb", [128, 128], BF16)
    xln_buf = [sb("xln_buf0", [128, 1024], F32), sb("xln_buf1", [128, 1024], F32)]

    wraw = [sb("wraw0", [128, 2816], F32), sb("wraw1", [128, 2816], F32)]
    wscr = sb("wscr", [128, 2816], F32)
    wint = sb("wint", [128, 2816], BF16)
    tstage = sb("tstage", [128, 1024], BF16)

    wgT_sb = [sb("wgT_sb0", [128, 1024], BF16), sb("wgT_sb1", [128, 1024], BF16)]
    wuT_sb = [sb("wuT_sb0", [128, 1024], BF16), sb("wuT_sb1", [128, 1024], BF16)]
    sil_sb = [sb("sil_sb0", [128, 512], F32), sb("sil_sb1", [128, 512], F32)]

    xq_tm = sb("xq_tm", [128, 1024], BF16)

    def addr(t):
        return nc.lookup_mloc(t).addr

    def sb_at(name, shape, dt, off):
        return nc.alloc_sbuf_tensor_at(name, shape, dt, offset=off)

    assert addr(wk_sb) == addr(wq_sb) + 16384 and addr(wv_sb) == addr(wq_sb) + 32768
    wdT_sb = sb_at("wdT_sb", [128, NJ * 1024], BF16, addr(wq_sb))
    assert addr(x_tm[0]) == addr(wo_sb) + 16384
    hqT = sb_at("hqT", [128, NJ * 1024], BF16, addr(wo_sb))
    assert addr(qht[1][0]) + 2048 - addr(wo_sb) >= NJ * 2048
    hp_in = [sb_at("hp_in0", [128, 1024], F16, addr(kfm[0])),
             sb_at("hp_in1", [128, 1024], F16, addr(kfm[0]) + 2048)]
    q_scr = [sb_at("q_scr0", [128, 1024], F32, addr(vfm[0])),
             sb_at("q_scr1", [128, 1024], F32, addr(vfm[1]))]
    junkf = sb_at("junkf", [128, 1024], F32, addr(wint))
    hqT2 = sb_at("hqT2", [128, NJ * 1024], BF16, addr(wraw[0]))
    assert addr(wgT_sb[1]) + 2048 - addr(wraw[0]) >= NJ * 2048
    xl_b = [sb_at("xl_b0", [128, 1024], F32, addr(xln_buf[0])),
            sb_at("xl_b1", [128, 1024], F32, addr(xln_buf[1]))]
    hp_st = [sb_at("hp_st0", [128, 512], F16, addr(avfm)),
             sb_at("hp_st1", [128, 512], F16, addr(avfm) + 1024),
             sb_at("hp_st2", [128, 512], F16, addr(xres)),
             sb_at("hp_st3", [128, 512], F16, addr(xres) + 1024)]
    assert addr(xln_buf[1]) == addr(xln_buf[0]) + 4096
    y_buf = [sb_at("y_buf0", [128, 1024], F32, addr(wuT_sb[0])),
             sb_at("y_buf1", [128, 1024], F32, addr(sil_sb[0]))]
    assert addr(wuT_sb[1]) == addr(wuT_sb[0]) + 2048
    assert addr(sil_sb[1]) == addr(sil_sb[0]) + 2048
    assert addr(wraw[1]) == addr(wraw[0]) + 11264
    xqT = [sb_at("xqT0", [128, 8 * 1024], BF16, addr(wraw[0])),
           sb_at("xqT1", [128, 8 * 1024], BF16, addr(wraw[0]) + 16384)]
    assert addr(wraw[0]) + 32768 <= addr(wint)
    assert addr(wint) + 5632 - addr(wraw[0]) >= 32768

    PS = es.enter_context(nc.psum_tensor("PS", [128, 4096], F32))
    PSB = PS.bitcast(BF16)

    sch = Sch(nc, es)
    for s in ("dx", "dw", "dwst", "dst1", "dlx", "dwg", "dhp", "dhl", "dout", "dmsc",
              "t", "v", "s", "sd", "cc", "g"):
        sch.sem(s)
    E = sch.emit
    now = sch.now

    def DR(eng):
        sch.emit(eng, lambda e: e.drain())

    E("sync", lambda e: e.dma_start(out=eyeF[:, :], in_=eye_f.ap()), incs=[("dmsc", 16)])
    E("sync", lambda e: e.dma_start(out=eyeB[:, :], in_=eye_b.ap()), incs=[("dmsc", 16)])
    E("sync", lambda e: e.dma_start(out=pswp[:, :], in_=pswap.ap()), incs=[("dmsc", 16)])
    E("sync", lambda e: e.dma_start(out=sel_sb[:, :], in_=sel_c.ap()), incs=[("dmsc", 16)])
    for g in range(8):
        E("sync", lambda e, g=g: e.dma_start(out=bqc[:, g:g + 1], in_=bq_h[g * 128:(g + 1) * 128]), incs=[("dmsc", 16)])
        E("sync", lambda e, g=g: e.dma_start(out=bkc[:, g:g + 1], in_=bk_h[g * 128:(g + 1) * 128]), incs=[("dmsc", 16)])
        E("sync", lambda e, g=g: e.dma_start(out=bvc[:, g:g + 1], in_=bv_h[g * 128:(g + 1) * 128]), incs=[("dmsc", 16)])
    E("sync", lambda e: e.dma_start(out=stg_f32[:, :], in_=bo_h[:]), incs=[("dmsc", 16)])
    E("sync", lambda e: e.dma_start(out=tiny[0:1, T_SIGX:T_SIGX + 1], in_=cx_h[:]), incs=[("dmsc", 16)])
    E("sync", lambda e: e.dma_start(out=tiny[0:1, T_SIGD:T_SIGD + 1], in_=cd_h[:]), incs=[("dmsc", 16)])
    dmsc_all = now("dmsc")
    for _eng in ("tensor", "vector", "scalar", "gpsimd"):
        sch.wait(_eng, ("dmsc", dmsc_all))

    E("vector", lambda e: e.memset(ones_r[:, :], 1.0))
    E("vector", lambda e: e.tensor_copy(bo_row[:, :], stg_f32[:, :]),
      waits=[("dmsc", dmsc_all)])
    E("vector", lambda e: e.memset(tiny[:, T_GMAXRUN:T_GMAXRUN + 1], 0.0))
    E("vector", lambda e: e.memset(tiny[:, T_RCP:T_RCP + 1], RC))
    E("vector", lambda e: e.memset(tiny[:, T_RCN:T_RCN + 1], -RC))
    E("vector", lambda e: e.memset(tiny[:, T_HMAXRUN:T_HMAXRUN + 1], 0.0), incs=[("v", 1)])

    E("scalar", lambda e: _load_act_set(e, AF.Sigmoid))
    E("scalar", lambda e: e.activation(tiny[0:1, T_SIGX:T_SIGX + 1], tiny[0:1, T_SIGX:T_SIGX + 1], AF.Sigmoid),
      waits=[("dmsc", dmsc_all)])
    E("scalar", lambda e: e.activation(tiny[0:1, T_SIGD:T_SIGD + 1], tiny[0:1, T_SIGD:T_SIGD + 1], AF.Sigmoid),
      incs=[("s", 1)])
    E("scalar", lambda e: _load_act_set(e, AF.Exp), incs=[("s", 1)])
    sig_done = now("s")

    sch._wraw_free = {0: ("v", 0), 1: ("v", 0)}
    for (w_h, w_sb) in ((wq_h, wq_sb), (wk_h, wk_sb), (wv_h, wv_sb), (wo_h, wo_sb)):
        for j in range(8):
            E("sync", lambda e, w_h=w_h, j=j: e.dma_start(
                out=wraw[j % 2][:, 0:1024], in_=w_h[j * 128:(j + 1) * 128, :]),
              waits=[sch._wraw_free[j % 2]], incs=[("dw", 16)])
            E("vector", lambda e, w_sb=w_sb, j=j: e.tensor_copy(
                w_sb[:, j * 1024:(j + 1) * 1024], wraw[j % 2][:, 0:1024]),
              waits=[("dw", now("dw"))], incs=[("v", 1)])
            sch._wraw_free[j % 2] = ("v", now("v"))

    prep_evac = {0: 0, 1: 0}
    sch._tstage_free = ("dwst", 0)

    bank6_prep_ref = [None]

    def prep_gu_tile(w_h, s_sb, out_h, j, part):
        if part == 0:
            E("sync", lambda e, w_h=w_h, j=j: e.dma_start(
                out=wraw[j % 2][:, 0:1024], in_=w_h[j * 128:(j + 1) * 128, :]),
              waits=[sch._wraw_free[j % 2]], incs=[("dw", 16)])
            dwn = now("dw")
            E("vector", lambda e, j=j: e.tensor_reduce(
                tiny[:, T_TMP1:T_TMP1 + 1], wraw[j % 2][:, 0:1024], AX.X, ALU.max,
                apply_absolute_value=True), waits=[("dw", dwn)])
            DR("vector")
            E("vector", lambda e, j=j, s_sb=s_sb: e.tensor_scalar(
                s_sb[:, j:j + 1], tiny[:, T_TMP1:T_TMP1 + 1], 1.0 / QMAX, None, ALU.mult))
            DR("vector")
            E("vector", lambda e, j=j, s_sb=s_sb: e.reciprocal(
                tiny[:, T_GUR:T_GUR + 1], s_sb[:, j:j + 1]))
            return
        if part == 1:
            DR("vector")
            E("vector", lambda e, j=j: e.tensor_scalar(
                wscr[:, 0:1024], wraw[j % 2][:, 0:1024], tiny[:, T_GUR:T_GUR + 1], None, ALU.mult),
              waits=[("t", now("t"))], incs=[("v", 1)])
            sch._wraw_free[j % 2] = ("v", now("v"))
            DR("vector")
            E("vector", lambda e: e.tensor_scalar(
                wscr[:, 0:1024], wscr[:, 0:1024], RC, RC, ALU.add, ALU.subtract))
            DR("vector")
            E("vector", lambda e: e.tensor_scalar(
                wint[:, 0:1024], wscr[:, 0:1024], QMAX, -QMAX - 1.0, ALU.min, ALU.max),
              incs=[("v", 1)])
            sch._gu_vq = now("v")
            return
        vq = sch._gu_vq
        b6 = bank6_prep_ref[0]
        for b in (0, 4):
            for hb in range(b, b + 4):
                pcol = 6144 + (hb % 4) * 256
                E("tensor", lambda e, hb=hb, pcol=pcol: e.transpose(
                    PSB[:, pcol:pcol + 128], wint[:, hb * 128:(hb + 1) * 128], eyeB[:, :]),
                  waits=[("v", vq), b6[0]] if hb == b else (),
                  incs=[("t", 1)] if hb == b + 3 else ())
            tv = now("t")
            for hb in range(b, b + 4):
                pcol = 6144 + (hb % 4) * 256
                wv_ = [("t", tv)] if hb == b else []
                if hb == 0:
                    wv_.append(sch._tstage_free)
                E("vector", lambda e, hb=hb, pcol=pcol: e.tensor_copy(
                    tstage[:, hb * 128:(hb + 1) * 128], PSB[:, pcol:pcol + 128]),
                  waits=wv_, incs=[("v", 1)] if hb == b + 3 else ())
            b6[0] = ("v", now("v"))
        E("gpsimd", lambda e, j=j, out_h=out_h: e.dma_start(
            out=out_h[j, :, :], in_=tstage[:, :]),
          waits=[("v", now("v"))], incs=[("dwst", 16)])
        sch._tstage_free = ("dwst", now("dwst"))

    GU_TILES = ([(wg_h, sgi, wgT_h, j) for j in range(NJ)]
                + [(wu_h, sui, wuT_h, j) for j in range(NJ)])

    sch._wd_slot_free = {k: ("dwst", 0) for k in range(8)}
    for hb in range(8):
        E("sync", lambda e, hb=hb: e.dma_start(
            out=wraw[hb % 2][:, :], in_=wd_h[hb * 128:(hb + 1) * 128, :]),
          waits=[sch._wraw_free[hb % 2]], incs=[("dw", 16)])
        dwn = now("dw")
        E("vector", lambda e, hb=hb: e.tensor_reduce(
            tiny[:, T_TMP1:T_TMP1 + 1], wraw[hb % 2][:, :], AX.X, ALU.max,
            apply_absolute_value=True), waits=[("dw", dwn), ("t", now("t"))])
        DR("vector")
        E("vector", lambda e: e.tensor_scalar(
            tiny[:, T_TMP2:T_TMP2 + 1], tiny[:, T_TMP1:T_TMP1 + 1], 1.0 / QMAX, None, ALU.mult))
        DR("vector")
        E("vector", lambda e: e.reciprocal(
            tiny[:, T_TMP1:T_TMP1 + 1], tiny[:, T_TMP2:T_TMP2 + 1]))
        DR("vector")
        E("vector", lambda e, hb=hb: e.tensor_scalar(
            wscr[:, :], wraw[hb % 2][:, :], tiny[:, T_TMP1:T_TMP1 + 1], None, ALU.mult),
          incs=[("v", 1)])
        sch._wraw_free[hb % 2] = ("v", now("v"))
        DR("vector")
        E("vector", lambda e: e.tensor_scalar(
            wscr[:, :], wscr[:, :], RC, RC, ALU.add, ALU.subtract))
        DR("vector")
        E("vector", lambda e: e.tensor_scalar(
            wscr[:, :], wscr[:, :], QMAX, -QMAX - 1.0, ALU.min, ALU.max))
        DR("vector")
        E("vector", lambda e: e.tensor_scalar(
            wint[:, :], wscr[:, :], tiny[:, T_TMP2:T_TMP2 + 1], None, ALU.mult),
          incs=[("v", 1)])
        vq = now("v")
        for j in range(NJ):
            slot = j % 2
            poff = 3072 + slot * 512
            E("tensor", lambda e, j=j, poff=poff: e.transpose(
                PSB[:, 2 * poff:2 * poff + 128], wint[:, j * 128:(j + 1) * 128], eyeB[:, :]),
              waits=[("v", vq), ("v", prep_evac[slot])], incs=[("t", 1)])
            E("vector", lambda e, j=j, poff=poff: e.tensor_copy(
                tstage[:, (j % 8) * 128:(j % 8) * 128 + 128], PSB[:, 2 * poff:2 * poff + 128]),
              waits=[("t", now("t")), sch._wd_slot_free[j % 8]], incs=[("v", 1)])
            prep_evac[slot] = now("v")
            E("gpsimd", lambda e, j=j, hb=hb: e.dma_start(
                out=wdT_h[j, :, hb * 128:(hb + 1) * 128], in_=tstage[:, (j % 8) * 128:(j % 8) * 128 + 128]),
              waits=[("v", now("v"))], incs=[("dwst", 16)])
            sch._wd_slot_free[j % 8] = ("dwst", now("dwst"))

    sch._tstage_free = ("dwst", now("dwst"))
    prep_v = now("v")
    prep_dw = now("dw")
    prep_dwst_ref = [now("dwst")]

    last_exp = {0: 0, 1: 0}
    last_diag = {0: 0, 1: 0}
    acc_free = {0: ("v", 0), 1: ("v", 0)}
    bank7_free = ("v", prep_v)
    bank6_last_evac = ("v", prep_v)
    sev_free = ("v", 0)
    zadd_v = 0
    xtm_free = {0: ("v", 0), 1: ("v", 0)}
    xln_free = {0: ("dst1", 0), 1: ("dst1", 0)}
    ACC = {0: 2048, 1: 2560}
    x_loaded = {}
    xfm_ready = {}
    negc_v = {}
    qkv_done = {}
    swap_done = {}
    qround = [0]

    def load_group(gi):
        buf = gi % 2
        for ci in range(2):
            c = 2 * gi + ci
            seq, pos = c // 16, (c % 16) * 128
            E("sync", lambda e, buf=buf, ci=ci, seq=seq, pos=pos: e.dma_start(
                out=x_tm[buf][:, ci * 1024:(ci + 1) * 1024],
                in_=x_h[seq, pos:pos + 128, :]),
              waits=[xtm_free[buf]], incs=[("dx", 16)])
        x_loaded[gi] = now("dx")

    bgq = []

    def bg_pop(n=1):
        for _ in range(n):
            if bgq:
                bgq.pop(0)()

    def bg_flush():
        while bgq:
            bgq.pop(0)()

    def emit_xt_batch(gi, ci, j0):
        nonlocal_state = None
        buf = gi % 2
        for j in range(j0, j0 + 4):
            slot = j % 4
            poff = 3072 + slot * 128
            E("tensor", lambda e, ci=ci, j=j, poff=poff, buf=buf: e.transpose(
                PS[:, poff:poff + 128],
                x_tm[buf][:, ci * 1024 + j * 128: ci * 1024 + (j + 1) * 128],
                eyeF[:, :]),
              waits=([("dx", x_loaded[gi]), bank6_last_evac_ref[0]]
                     if j == j0 else ()),
              incs=[("t", 1)] if j == j0 + 3 else ())
        tv = now("t")
        for j in range(j0, j0 + 4):
            slot = j % 4
            poff = 3072 + slot * 128
            E("vector", lambda e, ci=ci, j=j, poff=poff, buf=buf: e.tensor_copy(
                xfm[buf][:, j * 256 + ci * 128: j * 256 + (ci + 1) * 128],
                PS[:, poff:poff + 128]),
              waits=[("t", tv)] if j == j0 else (),
              incs=[("v", 1)] if j == j0 + 3 else ())
        bank6_last_evac_ref[0] = ("v", now("v"))

    bank6_last_evac_ref = [bank6_last_evac]
    bank6_prep_ref[0] = bank6_last_evac_ref

    def emit_qkv_round(gi, kind, idx):
        buf = gi % 2
        bank = qround[0] % 2
        qround[0] += 1
        aoff = ACC[bank]
        w0 = [("v", xfm_ready[gi]), acc_free[bank]]
        if kind == "q":
            for j in range(8):
                E("tensor", lambda e, j=j, idx=idx, aoff=aoff, buf=buf: e.matmul(
                    PS[0:64, aoff:aoff + 256],
                    wq_sb[:, j * 1024 + idx * 128: j * 1024 + idx * 128 + 64],
                    xfm[buf][:, j * 256:(j + 1) * 256],
                    start=(j == 0), stop=(j == 7), tile_position=(0, 0)),
                  waits=w0 if j == 0 else ())
                E("tensor", lambda e, j=j, idx=idx, aoff=aoff, buf=buf: e.matmul(
                    PS[64:128, aoff:aoff + 256],
                    wq_sb[:, j * 1024 + idx * 128 + 64: j * 1024 + (idx + 1) * 128],
                    xfm[buf][:, j * 256:(j + 1) * 256],
                    start=(j == 0), stop=(j == 7), tile_position=(0, 64)),
                  incs=[("t", 1)] if j == 7 else ())
        else:
            wsb = wk_sb if kind == "k" else wv_sb
            for j in range(8):
                E("tensor", lambda e, j=j, idx=idx, aoff=aoff, wsb=wsb, buf=buf: e.matmul(
                    PS[:, aoff:aoff + 256],
                    wsb[:, j * 1024 + idx * 128: j * 1024 + (idx + 1) * 128],
                    xfm[buf][:, j * 256:(j + 1) * 256],
                    start=(j == 0), stop=(j == 7)),
                  waits=w0 if j == 0 else (), incs=[("t", 1)] if j == 7 else ())
        tv = now("t")
        for ci in range(2):
            if kind == "q":
                fn = lambda e, ci=ci, idx=idx, aoff=aoff, buf=buf: e.tensor_scalar(
                    qht[buf][ci][:, idx * 128:(idx + 1) * 128],
                    PS[:, aoff + ci * 128: aoff + (ci + 1) * 128],
                    bqc[:, idx:idx + 1], INV32, ALU.add, ALU.mult)
            elif kind == "k":
                fn = lambda e, ci=ci, idx=idx, aoff=aoff, buf=buf: e.tensor_scalar(
                    kfm[buf][:, idx * 256 + ci * 128: idx * 256 + (ci + 1) * 128],
                    PS[:, aoff + ci * 128: aoff + (ci + 1) * 128],
                    bkc[:, idx:idx + 1], INV32, ALU.add, ALU.mult)
            else:
                fn = lambda e, ci=ci, idx=idx, aoff=aoff, buf=buf: e.tensor_scalar(
                    vfm[buf][:, idx * 256 + ci * 128: idx * 256 + (ci + 1) * 128],
                    PS[:, aoff + ci * 128: aoff + (ci + 1) * 128],
                    bvc[:, idx:idx + 1], None, ALU.add)
            E("vector", fn, waits=[("t", tv)] if ci == 0 else (),
              incs=[("v", 1)] if ci == 1 else ())
        acc_free[bank] = ("v", now("v"))

    def emit_swap(gi, ci, hh):
        buf = gi % 2
        bank = hh
        aoff = ACC[bank]
        E("tensor", lambda e, hh=hh, ci=ci, aoff=aoff, buf=buf: e.matmul(
            PS[:, aoff:aoff + 512],
            pswp[:, :], qht[buf][ci][:, hh * 512:(hh + 1) * 512],
            start=True, stop=True),
          waits=[("v", qkv_done[gi]), acc_free[bank]], incs=[("t", 1)])
        E("vector", lambda e, hh=hh, ci=ci, aoff=aoff, buf=buf: e.tensor_copy(
            qhts[buf][ci][:, hh * 512:(hh + 1) * 512], PS[:, aoff:aoff + 512]),
          waits=[("t", now("t"))])
        E("vector", lambda e: e.drain(), incs=[("v", 1)])
        acc_free[bank] = ("v", now("v"))
        swap_done[gi] = now("v")

    def fill_bg(gi):
        n0 = len(bgq)
        for ci in range(2):
            for j0 in (0, 4):
                bgq.append(lambda ci=ci, j0=j0: emit_xt_batch(gi, ci, j0))
        def _xfm_mark(gi=gi):
            E("vector", lambda e: e.drain(), incs=[("v", 1)])
            xfm_ready[gi] = now("v")
        bgq.append(_xfm_mark)
        for kind in ("q", "k", "vv"):
            for idx in range(8):
                bgq.append(lambda kind=kind, idx=idx: emit_qkv_round(gi, kind, idx))
        def _qkv_mark(gi=gi):
            E("vector", lambda e: e.drain(), incs=[("v", 1)])
            qkv_done[gi] = now("v")
        bgq.append(_qkv_mark)
        bgq.append(lambda: emit_bound_vec(gi, 0))
        bgq.append(lambda: emit_bound_vec(gi, 1))
        for ci in range(2):
            for hh in range(2):
                bgq.append(lambda ci=ci, hh=hh: emit_swap(gi, ci, hh))
                if ci == 0 and hh == 0:
                    bgq.append(lambda: emit_bound_vec(gi, 2))
                elif ci == 0 and hh == 1:
                    bgq.append(lambda: emit_bound_vec(gi, 3))
        bgq.append(lambda: emit_bound_mm(gi))
        if 1 <= gi <= 11:
            gu_items = []
            for t in range(4):
                idx = (gi - 1) * 4 + t
                if idx < len(GU_TILES):
                    for part in range(3):
                        def _gu(idx=idx, part=part):
                            prep_gu_tile(*GU_TILES[idx], part)
                            if idx == len(GU_TILES) - 1 and part == 2:
                                prep_dwst_ref[0] = now("dwst")
                        gu_items.append(_gu)
            grp = bgq[n0:]
            del bgq[n0:]
            k = 0
            for i, it in enumerate(grp):
                bgq.append(it)
                if k < len(gu_items) and i % 3 == 2:
                    bgq.append(gu_items[k])
                    k += 1
            bgq.extend(gu_items[k:])

    bound_vec_v = {}

    def emit_bound_vec(gi, part):
        buf = gi % 2
        if part == 0:
            E("vector", lambda e, buf=buf: e.tensor_reduce(
                tiny[:, T_MA:T_MA + 1], qht[buf][0][:, :], AX.X, ALU.max,
                apply_absolute_value=True),
              waits=[("v", now("v"))])
            return
        DR("vector")
        if part == 1:
            src = lambda: qht[buf][1][:, :]
            E("vector", lambda e, buf=buf: e.tensor_reduce(
                tiny[:, T_TMP1:T_TMP1 + 1], qht[buf][1][:, :], AX.X, ALU.max,
                apply_absolute_value=True))
        elif part == 2:
            E("vector", lambda e, buf=buf: e.tensor_reduce(
                tiny[:, T_TMP1:T_TMP1 + 1], kfm[buf][:, 0:1024], AX.X, ALU.max,
                apply_absolute_value=True))
        else:
            E("vector", lambda e, buf=buf: e.tensor_reduce(
                tiny[:, T_TMP1:T_TMP1 + 1], kfm[buf][:, 1024:2048], AX.X, ALU.max,
                apply_absolute_value=True))
        DR("vector")
        E("vector", lambda e: e.tensor_tensor(
            tiny[:, T_MA:T_MA + 1], tiny[:, T_MA:T_MA + 1], tiny[:, T_TMP1:T_TMP1 + 1], ALU.max))
        if part == 3:
            E("vector", lambda e: e.drain(), incs=[("v", 1)])
            bound_vec_v[gi] = now("v")

    def emit_bound_mm(gi):
        buf = gi % 2
        E("tensor", lambda e: e.matmul(
            PS[0:1, 3584:3712], tiny[:, T_MA:T_MA + 1], eyeF[:, :], start=True, stop=True),
          waits=[("v", bound_vec_v[gi]), bank7_free_ref[0]], incs=[("t", 1)])
        E("vector", lambda e: e.tensor_reduce(
            tiny[0:1, T_MA:T_MA + 1], PS[0:1, 3584:3712], AX.X, ALU.max),
          waits=[("t", now("t"))])
        DR("vector")
        E("vector", lambda e: e.tensor_tensor(
            tiny[0:1, T_TMP1:T_TMP1 + 1], tiny[0:1, T_MA:T_MA + 1], tiny[0:1, T_MA:T_MA + 1], ALU.mult))
        DR("vector")
        E("vector", lambda e: e.tensor_scalar(
            tiny[0:1, T_NEGC:T_NEGC + 1], tiny[0:1, T_TMP1:T_TMP1 + 1], -64.0, None, ALU.mult))
        E("vector", lambda e: e.memset(stg_f32[0:1, 0:128], 1.0))
        E("vector", lambda e: e.drain(), incs=[("v", 1)])
        E("tensor", lambda e: e.matmul(
            PS[0:128, 3712:3713], stg_f32[0:1, 0:128], tiny[0:1, T_NEGC:T_NEGC + 1],
            start=True, stop=True),
          waits=[("v", now("v"))], incs=[("t", 1)])
        E("vector", lambda e, buf=buf: e.tensor_copy(negc_col[buf][:, :], PS[:, 3712:3713]),
          waits=[("t", now("t"))])
        E("vector", lambda e: e.drain(), incs=[("v", 1)])
        negc_v[gi] = now("v")
        bank7_free_ref[0] = ("v", negc_v[gi])

    bank7_free_ref = [bank7_free]

    load_group(0)
    load_group(1)
    fill_bg(0)
    bg_flush()
    fill_bg(1)

    pend = [None]

    def make_pend(gi, ci, diag_v_c, zadd_v_c):
        buf = gi % 2
        c = 2 * gi + ci
        st = {}

        def p_diag():
            E("tensor", lambda e: e.matmul(
                PS[0:16, 3584:3712], diag_tm[:, :], eyeF[:, :], start=True, stop=True),
              waits=[("v", diag_v_c), bank7_free_ref[0]], incs=[("t", 1)])
            E("vector", lambda e: e.tensor_copy(diagT_sb[:, :], PS[0:16, 3584:3712]),
              waits=[("t", now("t"))])
            E("vector", lambda e: e.drain(), incs=[("v", 1)])
            diagT_v = now("v")
            bank7_free_ref[0] = ("v", diagT_v)
            for fb in range(16):
                ft, par = fb // 2, fb % 2
                boff = par * 64
                E("tensor", lambda e, fb=fb, ft=ft, boff=boff: e.matmul(
                    PS[boff:boff + 64, 1024 + ft * 128: 1024 + (ft + 1) * 128],
                    sel_sb[:, fb * 64:(fb + 1) * 64], diagT_sb[:, :],
                    start=True, stop=True),
                  waits=[("v", diagT_v), ("s", st["lexp1"])] if fb == 0 else (),
                  incs=[("t", 1)] if fb == 15 else ())
            dbc_t = now("t")
            for ft in range(8):
                E("vector", lambda e, ft=ft, ci=ci, buf=buf: e.tensor_tensor(
                    avfm[:, ft * 128:(ft + 1) * 128],
                    vfm[buf][:, ft * 256 + ci * 128: ft * 256 + (ci + 1) * 128],
                    PS[:, 1024 + ft * 128: 1024 + (ft + 1) * 128], ALU.mult),
                  waits=[("t", dbc_t)] if ft == 0 else ())
            E("vector", lambda e: e.drain(), incs=[("v", 1)])
            st["av_v"] = now("v")

        def p_o(fh):
            aoff = ACC[fh]
            for j in range(8):
                E("tensor", lambda e, j=j, fh=fh, aoff=aoff: e.matmul(
                    PS[:, aoff:aoff + 512],
                    avfm[:, j * 128:(j + 1) * 128],
                    wo_sb[:, j * 1024 + fh * 512: j * 1024 + (fh + 1) * 512],
                    start=(j == 0), stop=False),
                  waits=[("v", st["av_v"]), acc_free[fh]] if j == 0 else ())
            E("tensor", lambda e, fh=fh, aoff=aoff: e.matmul(
                PS[:, aoff:aoff + 512],
                ones_r[0:1, :], bo_row[0:1, fh * 512:(fh + 1) * 512],
                start=False, stop=True), incs=[("t", 1)])
            st["o_t%d" % fh] = now("t")

        def p_ln():
            for fh in range(2):
                aoff = ACC[fh]
                E("vector", lambda e, fh=fh, aoff=aoff, buf=buf, ci=ci: e.scalar_tensor_tensor(
                    xres[:, fh * 512:(fh + 1) * 512], PS[:, aoff:aoff + 512],
                    1.0, x_tm[buf][:, ci * 1024 + fh * 512: ci * 1024 + (fh + 1) * 512],
                    ALU.mult, ALU.add,
                    accum_out=tiny[:, T_S1A + fh:T_S1A + fh + 1]),
                  waits=[("t", st["o_t%d" % fh])], incs=[("v", 1)])
                acc_free[fh] = ("v", now("v"))
            DR("vector")
            E("vector", lambda e: e.tensor_tensor(
                tiny[:, T_S1:T_S1 + 1], tiny[:, T_S1A:T_S1A + 1], tiny[:, T_S1B:T_S1B + 1], ALU.add))
            E("vector", lambda e, buf=buf, ci=ci: e.scalar_tensor_tensor(
                x_tm[buf][:, ci * 1024:(ci + 1) * 1024], xres[:, :], 1.0, xres[:, :],
                ALU.mult, ALU.mult, accum_out=tiny[:, T_S2:T_S2 + 1]))
            DR("vector")
            E("vector", lambda e: e.tensor_scalar(
                tiny[:, T_NEGMEAN:T_NEGMEAN + 1], tiny[:, T_S1:T_S1 + 1], -1.0 / D, None, ALU.mult))
            E("vector", lambda e: e.tensor_scalar(
                tiny[:, T_VP:T_VP + 1], tiny[:, T_S2:T_S2 + 1], 1.0 / D, EPS, ALU.mult, ALU.add))
            DR("vector")
            E("vector", lambda e: e.tensor_tensor(
                tiny[:, T_TMP1:T_TMP1 + 1], tiny[:, T_NEGMEAN:T_NEGMEAN + 1],
                tiny[:, T_NEGMEAN:T_NEGMEAN + 1], ALU.mult))
            DR("vector")
            E("vector", lambda e: e.tensor_tensor(
                tiny[:, T_VP:T_VP + 1], tiny[:, T_VP:T_VP + 1], tiny[:, T_TMP1:T_TMP1 + 1],
                ALU.subtract), incs=[("v", 1)])
            E("scalar", lambda e: e.activation(
                tiny[:, T_TMP2:T_TMP2 + 1], tiny[:, T_VP:T_VP + 1], AF.Ln),
              waits=[("v", now("v"))])
            DR("scalar")
            E("scalar", lambda e: e.activation(
                tiny[:, T_RSTD:T_RSTD + 1], tiny[:, T_TMP2:T_TMP2 + 1], AF.Exp, scale=-0.5),
              incs=[("s", 1)])
            rstd_s = now("s")
            xb = c % 2
            E("vector", lambda e, xb=xb: e.tensor_scalar(
                xln_buf[xb][:, :], xres[:, :],
                tiny[:, T_NEGMEAN:T_NEGMEAN + 1], tiny[:, T_RSTD:T_RSTD + 1],
                ALU.add, ALU.mult),
              waits=[("s", rstd_s), xln_free[xb]])
            DR("vector")
            E("vector", lambda e, xb=xb: e.tensor_reduce(
                tiny[:, T_TMP1:T_TMP1 + 1], xln_buf[xb][:, :], AX.X, ALU.max,
                apply_absolute_value=True))
            DR("vector")
            E("vector", lambda e: e.tensor_tensor(
                tiny[:, T_GMAXRUN:T_GMAXRUN + 1], tiny[:, T_GMAXRUN:T_GMAXRUN + 1],
                tiny[:, T_TMP1:T_TMP1 + 1], ALU.max), incs=[("v", 1)])
            E("sync", lambda e, xb=xb, c=c: e.dma_start(
                out=xln1_h[c, :, :], in_=xln_buf[xb][:, :]),
              waits=[("v", now("v"))], incs=[("dst1", 16)])
            xln_free[xb] = ("dst1", now("dst1"))
            st["done_v"] = now("v")

        return {"diag": p_diag, "o0": lambda: p_o(0), "o1": lambda: p_o(1),
                "ln": p_ln, "st": st}

    def drain_pend():
        if pend[0] is not None:
            p = pend[0]
            p["diag"](); p["o0"](); p["o1"](); p["ln"]()
            pend[0] = None
            return p["st"]
        return None

    for gi in range(32):
        buf = gi % 2
        for ci in range(2):
            c = 2 * gi + ci

            swap_v = swap_done[gi]

            chunk_pops = [0]
            if pend[0] is not None:
                pend[0]["diag"]()
                sev_free = ("v", pend[0]["st"]["av_v"])
            for fb in range(16):
                ft, par = fb // 2, fb % 2
                boff = par * 64
                for half in range(2):
                    soff = half * 1024
                    if half == 0:
                        srcb = qht[buf][ci] if par == 0 else qhts[buf][ci]
                        uses_swap = par == 1
                    else:
                        srcb = qhts[buf][ci] if par == 0 else qht[buf][ci]
                        uses_swap = par == 0
                    w = [("s", last_exp[half])]
                    if uses_swap:
                        w.append(("v", swap_v))
                    elif fb <= 1:
                        w.append(("v", qkv_done[gi]))
                    if half == 0 and fb == 0:
                        w.append(sev_free)
                    for nn in range(2):
                        E("tensor", lambda e, ft=ft, boff=boff, soff=soff, srcb=srcb, nn=nn, ci=ci, buf=buf: e.matmul(
                            PS[:, soff + nn * 512: soff + (nn + 1) * 512],
                            kfm[buf][boff:boff + 64, ft * 256 + ci * 128: ft * 256 + (ci + 1) * 128],
                            srcb[boff:boff + 64, nn * 512:(nn + 1) * 512],
                            start=True, stop=True),
                          waits=w if nn == 0 else (), incs=[("t", 1)] if nn == 1 else ())
                    tv = now("t")
                    zb = (c % 2) * 80
                    zcol = zb + 16 * half + fb
                    w2 = [("t", tv), ("v", last_diag[half]), ("v", negc_v[gi])]
                    E("scalar", lambda e, half=half, soff=soff, zcol=zcol, buf=buf: e.activation(
                        e_sb[half][:, :], PS[:, soff:soff + 1024], AF.Exp,
                        bias=negc_col[buf][:, :], scale=1.0,
                        accum_out=zzt[:, zcol:zcol + 1]),
                      waits=w2, incs=[("s", 1)])
                    last_exp[half] = now("s")
                    if half == par:
                        E("scalar", lambda e: e.drain(), incs=[("sd", 1)])
                        E("vector", lambda e, ft=ft, fb=fb, half=half, zb=zb: e.scalar_tensor_tensor(
                            junkb[:, :], e_sb[half][:, ft * 128:(ft + 1) * 128],
                            1.0, eyeB[:, :], ALU.mult, ALU.mult,
                            accum_out=zzt[:, zb + 32 + fb:zb + 32 + fb + 1]),
                          waits=[("sd", now("sd"))], incs=[("v", 1)])
                        last_diag[half] = now("v")
                    slot = fb * 2 + half
                    if slot == 0 and pend[0] is not None:
                        pend[0]["o0"]()
                    elif slot == 1 and pend[0] is not None:
                        pend[0]["o1"]()
                    elif slot == 2 and pend[0] is not None:
                        pend[0]["ln"]()
                        pend[0] = None
                    elif slot >= 3 and chunk_pops[0] < 29:
                        bg_pop(1)
                        chunk_pops[0] += 1

            zb = (c % 2) * 80
            E("scalar", lambda e: e.drain(), waits=[("s", last_exp[1])], incs=[("s", 1)])
            E("vector", lambda e, zb=zb: e.tensor_tensor(
                zzt[:, zb + 48:zb + 64], zzt[:, zb + 0:zb + 16], zzt[:, zb + 16:zb + 32], ALU.add),
              waits=[("s", now("s"))], incs=[("v", 1)])
            zadd_v = now("v")
            DR("vector")
            E("vector", lambda e, zb=zb: e.tensor_scalar(
                zzt[:, zb + 48:zb + 64], zzt[:, zb + 48:zb + 64], 1e-30, None, ALU.max))
            DR("vector")
            E("vector", lambda e, zb=zb: e.reciprocal(zzt[:, zb + 64:zb + 80], zzt[:, zb + 48:zb + 64]))
            DR("vector")
            E("vector", lambda e, zb=zb: e.tensor_tensor(
                diag_tm[:, :], zzt[:, zb + 32:zb + 48], zzt[:, zb + 64:zb + 80], ALU.mult))
            E("vector", lambda e: e.drain(), incs=[("v", 1)])
            p = make_pend(gi, ci, now("v"), zadd_v)
            p["st"]["lexp1"] = last_exp[1]
            pend[0] = p

            if ci == 0 and gi >= 1 and gi + 1 <= 31:
                xtm_free[(gi - 1) % 2] = ("v", now("v"))
                load_group(gi + 1)
                fill_bg(gi + 1)

            if debug and gi == 0 and ci == 0:
                drain_pend()
                vv = now("v")
                for nm, src_t in (("d_xfm", xfm[0]), ("d_qht", qht[0][0]), ("d_qhts", qhts[0][0]),
                                  ("d_kfm", kfm[0]), ("d_vfm", vfm[0]),
                                  ("d_negc", negc_col[0]), ("d_zzt", zzt), ("d_diag", diag_tm),
                                  ("d_av", avfm), ("d_xres", xres), ("d_xln", xln_buf[0])):
                    E("sync", lambda e, nm=nm, src_t=src_t: e.dma_start(
                        out=dbg[nm][:, :], in_=src_t[:, :]),
                      waits=[("v", vv)], incs=[("dmsc", 16)])
                for eng in ("vector", "scalar", "tensor", "sync"):
                    sch.wait(eng, ("dmsc", now("dmsc")))
        bg_flush()

    stf = drain_pend()
    m67_free = bank7_free_ref[0]

    E("tensor", lambda e: e.matmul(
        PS[0:1, 3584:3712], tiny[:, T_GMAXRUN:T_GMAXRUN + 1], eyeF[:, :],
        start=True, stop=True),
      waits=[("v", now("v")), m67_free], incs=[("t", 1)])
    E("vector", lambda e: e.tensor_reduce(
        tiny[0:1, T_G11:T_G11 + 1], PS[0:1, 3584:3712], AX.X, ALU.max),
      waits=[("t", now("t"))], incs=[("v", 1)])
    E("gpsimd", lambda e: e.dma_start(out=cc1i_h[:], in_=tiny[0:1, T_G11:T_G11 + 1]),
      waits=[("v", now("v"))], incs=[("g", 16)])
    E("gpsimd", lambda e: e.collective_compute(
        "AllReduce", ALU.max, replica_groups=[list(range(NCORES))],
        ins=[cc1i_h.opt()], outs=[cc1o_h.opt()]),
      waits=[("g", now("g"))], incs=[("cc", 1)])
    E("gpsimd", lambda e: e.dma_start(out=tiny[0:1, T_G11:T_G11 + 1], in_=cc1o_h[:]),
      waits=[("cc", now("cc"))], incs=[("g", 16)])
    gmax_g = now("g")
    E("vector", lambda e: e.tensor_tensor(
        tiny[0:1, T_TMP1:T_TMP1 + 1], tiny[0:1, T_G11:T_G11 + 1],
        tiny[0:1, T_SIGX:T_SIGX + 1], ALU.mult),
      waits=[("g", gmax_g), ("s", sig_done)])
    DR("vector")
    E("vector", lambda e: e.tensor_scalar(
        tiny[0:1, T_SX:T_SX + 1], tiny[0:1, T_TMP1:T_TMP1 + 1], 1.0 / QMAX, None, ALU.mult))
    DR("vector")
    E("vector", lambda e: e.reciprocal(
        tiny[0:1, T_RX:T_RX + 1], tiny[0:1, T_SX:T_SX + 1]))
    E("vector", lambda e: e.memset(stg_f32[0:1, 0:128], 1.0), incs=[("v", 1)])
    E("tensor", lambda e: e.matmul(
        PS[0:128, 3584:3585], stg_f32[0:1, 0:128], tiny[0:1, T_RX:T_RX + 1],
        start=True, stop=True), waits=[("v", now("v"))])
    E("tensor", lambda e: e.matmul(
        PS[0:128, 3585:3586], stg_f32[0:1, 0:128], tiny[0:1, T_SX:T_SX + 1],
        start=True, stop=True), incs=[("t", 1)])
    E("vector", lambda e: e.tensor_copy(rx_col[:, :], PS[:, 3584:3585]),
      waits=[("t", now("t"))])
    E("vector", lambda e: e.tensor_copy(sx_col[:, :], PS[:, 3585:3586]))
    DR("vector")
    E("vector", lambda e: e.tensor_scalar(
        sg_eff[:, :], sgi[:, :], sx_col[:, 0:1], None, ALU.mult))
    E("vector", lambda e: e.tensor_scalar(
        su_eff[:, :], sui[:, :], sx_col[:, 0:1], None, ALU.mult), incs=[("v", 1)])
    bar1_v = now("v")
    E("scalar", lambda e: _load_act_set(e, AF.Silu), waits=[("v", bar1_v)], incs=[("s", 1)])
    silu_loaded = now("s")

    xln_stored = now("dst1")
    hp_free = {k: ("dhp", 0) for k in range(4)}
    xqT_free = {0: ("t", 0), 1: ("t", 0)}
    gu_free = {k: ("v", 0) for k in range(4)}
    wgbuf_free = {0: ("t", 0), 1: ("t", 0)}
    xl_free = {0: ("v", 0), 1: ("v", 0)}
    sil_free = {0: ("v", 0), 1: ("v", 0)}
    xq_evac = {}
    GU_OFF = {0: 0, 1: 512, 2: 1024, 3: 1536}

    def emit_xq(tb, tbuf):
        for k in range(8):
            c = tb * 8 + k
            lb = c % 2
            E("scalar", lambda e, c=c, lb=lb: e.dma_start(
                out=xl_b[lb][:, :], in_=xln1_h[c, :, :]),
              waits=[("dst1", xln_stored), xl_free[lb]], incs=[("dlx", 16)])
            dl = now("dlx")
            E("vector", lambda e, lb=lb: e.tensor_scalar(
                junkf[:, :], xl_b[lb][:, :], rx_col[:, 0:1], None, ALU.mult),
              waits=[("dlx", dl), ("v", bar1_v)], incs=[("v", 1)])
            xl_free[lb] = ("v", now("v"))
            DR("vector")
            E("vector", lambda e: e.tensor_scalar(
                junkf[:, :], junkf[:, :], RC, RC, ALU.add, ALU.subtract))
            DR("vector")
            E("vector", lambda e: e.tensor_scalar(
                xq_tm[:, :], junkf[:, :], QMAX, -QMAX - 1.0, ALU.min, ALU.max),
              incs=[("v", 1)])
            vq = now("v")
            for b in (0, 4):
                for hb in range(b, b + 4):
                    slot = hb % 4
                    poff = 3072 + (slot % 2) * 512 + (slot // 2) * 128
                    w = ()
                    if hb == b:
                        w = [("v", vq), ("v", xq_evac.get(0, 0))]
                        if k == 0 and hb == 0:
                            w.append(xqT_free[tbuf])
                    E("tensor", lambda e, hb=hb, poff=poff: e.transpose(
                        PSB[:, 2 * poff:2 * poff + 128], xq_tm[:, hb * 128:(hb + 1) * 128], eyeB[:, :]),
                      waits=w, incs=[("t", 1)] if hb == b + 3 else ())
                tv = now("t")
                for hb in range(b, b + 4):
                    slot = hb % 4
                    poff = 3072 + (slot % 2) * 512 + (slot // 2) * 128
                    E("vector", lambda e, hb=hb, k=k, poff=poff, tbuf=tbuf: e.tensor_copy(
                        xqT[tbuf][:, hb * 1024 + k * 128: hb * 1024 + (k + 1) * 128],
                        PSB[:, 2 * poff:2 * poff + 128]),
                      waits=[("t", tv)] if hb == b else (), incs=[("v", 1)] if hb == b + 3 else ())
                xq_evac[0] = now("v")
        return now("v")

    if debug:
        E("sync", lambda e: e.dma_start(out=dbg["d_gmax"][:, :], in_=tiny[0:1, T_G11:T_G11 + 1]),
          waits=[("v", bar1_v)], incs=[("dmsc", 16)])
        E("sync", lambda e: e.dma_start(out=dbg["d_rx"][:, :], in_=rx_col[:, :]),
          waits=[("v", bar1_v)], incs=[("dmsc", 16)])
        E("sync", lambda e: e.dma_start(out=dbg["d_sgi"][:, :], in_=sgi[:, :]),
          waits=[("v", bar1_v)], incs=[("dmsc", 16)])
        for eng in ("vector", "scalar", "tensor", "sync"):
            sch.wait(eng, ("dmsc", now("dmsc")))
    xq_ready = {0: emit_xq(0, 0)}
    if debug:
        E("sync", lambda e: e.dma_start(out=dbg["d_xq"][:, :],
                                        in_=xq_tm[:, :]),
          waits=[("v", xq_ready[0])], incs=[("dmsc", 16)])
        for eng in ("vector", "scalar", "tensor", "sync"):
            sch.wait(eng, ("dmsc", now("dmsc")))
    hp_seq = 0
    for tb in range(8):
        tbuf = tb % 2
        xqv = xq_ready[tb]
        for j in range(NJ):
            if j == 3 and tb + 1 < 8:
                xq_ready[tb + 1] = emit_xq(tb + 1, (tb + 1) % 2)
            wb = j % 2
            E("sync", lambda e, j=j, wb=wb: e.dma_start(
                out=wgT_sb[wb][:, :], in_=wgT_h[j, :, :]),
              waits=[("dwst", prep_dwst_ref[0]), wgbuf_free[wb]], incs=[("dwg", 16)])
            E("sync", lambda e, j=j, wb=wb: e.dma_start(
                out=wuT_sb[wb][:, :], in_=wuT_h[j, :, :]), incs=[("dwg", 16)])
            dwgn = now("dwg")
            for th in range(2):
                goff = GU_OFF[2 * th]
                uoff = GU_OFF[2 * th + 1]
                for hb in range(8):
                    E("tensor", lambda e, hb=hb, th=th, wb=wb, tbuf=tbuf, goff=goff: e.matmul(
                        PS[:, goff:goff + 512],
                        wgT_sb[wb][:, hb * 128:(hb + 1) * 128],
                        xqT[tbuf][:, hb * 1024 + th * 512: hb * 1024 + (th + 1) * 512],
                        start=(hb == 0), stop=(hb == 7)),
                      waits=[("dwg", dwgn), ("v", xqv), gu_free[2 * th]] if hb == 0 else (),
                      incs=[("t", 1)] if hb == 7 else ())
                gt = now("t")
                for hb in range(8):
                    E("tensor", lambda e, hb=hb, th=th, wb=wb, tbuf=tbuf, uoff=uoff: e.matmul(
                        PS[:, uoff:uoff + 512],
                        wuT_sb[wb][:, hb * 128:(hb + 1) * 128],
                        xqT[tbuf][:, hb * 1024 + th * 512: hb * 1024 + (th + 1) * 512],
                        start=(hb == 0), stop=(hb == 7)),
                      waits=[gu_free[2 * th + 1]] if hb == 0 else (),
                      incs=[("t", 1)] if hb == 7 else ())
                ut = now("t")
                sbi = th
                E("scalar", lambda e, sbi=sbi, j=j, goff=goff: e.activation(
                    sil_sb[sbi][:, :], PS[:, goff:goff + 512], AF.Silu,
                    bias=0.0, scale=sg_eff[:, j:j + 1]),
                  waits=[("t", gt), ("s", silu_loaded), sil_free[sbi]], incs=[("s", 1)])
                sl_s = now("s")
                gu_free[2 * th] = ("s", sl_s)
                hbi = hp_seq % 4
                hp_seq += 1
                E("vector", lambda e, hbi=hbi, sbi=sbi, j=j, uoff=uoff: e.scalar_tensor_tensor(
                    hp_st[hbi][:, :], PS[:, uoff:uoff + 512],
                    su_eff[:, j:j + 1], sil_sb[sbi][:, :], ALU.mult, ALU.mult),
                  waits=[("t", ut), ("s", sl_s), hp_free[hbi]], incs=[("v", 1)])
                gu_free[2 * th + 1] = ("v", now("v"))
                sil_free[sbi] = ("v", now("v"))
                DR("vector")
                E("vector", lambda e, hbi=hbi: e.tensor_reduce(
                    tiny[:, T_TMP1:T_TMP1 + 1], hp_st[hbi][:, :], AX.X, ALU.max,
                    apply_absolute_value=True))
                DR("vector")
                E("vector", lambda e: e.tensor_tensor(
                    tiny[:, T_HMAXRUN:T_HMAXRUN + 1], tiny[:, T_HMAXRUN:T_HMAXRUN + 1],
                    tiny[:, T_TMP1:T_TMP1 + 1], ALU.max), incs=[("v", 1)])
                E("gpsimd", lambda e, hbi=hbi, tb=tb, th=th, j=j: e.dma_start(
                    out=hp_h[2 * tb + th, j, :, :], in_=hp_st[hbi][:, :]),
                  waits=[("v", now("v"))], incs=[("dhp", 16)])
                hp_free[hbi] = ("dhp", now("dhp"))
                if debug and tb == 0 and th == 0 and j == 0:
                    E("sync", lambda e, hbi=hbi: e.dma_start(
                        out=dbg["d_hp"][:, :], in_=hp_st[hbi][:, :]),
                      waits=[("v", now("v"))], incs=[("dmsc", 16)])
                    for eng in ("vector", "scalar", "tensor", "sync"):
                        sch.wait(eng, ("dmsc", now("dmsc")))
            wgbuf_free[wb] = ("t", now("t"))
        xqT_free[tbuf] = ("t", now("t"))

    E("tensor", lambda e: e.matmul(
        PS[0:1, 3584:3712], tiny[:, T_HMAXRUN:T_HMAXRUN + 1], eyeF[:, :],
        start=True, stop=True), waits=[("v", now("v"))], incs=[("t", 1)])
    E("vector", lambda e: e.tensor_reduce(
        tiny[0:1, T_G11:T_G11 + 1], PS[0:1, 3584:3712], AX.X, ALU.max),
      waits=[("t", now("t"))], incs=[("v", 1)])
    E("gpsimd", lambda e: e.dma_start(out=cc2i_h[:], in_=tiny[0:1, T_G11:T_G11 + 1]),
      waits=[("v", now("v"))], incs=[("g", 16)])
    E("gpsimd", lambda e: e.collective_compute(
        "AllReduce", ALU.max, replica_groups=[list(range(NCORES))],
        ins=[cc2i_h.opt()], outs=[cc2o_h.opt()]),
      waits=[("g", now("g"))], incs=[("cc", 1)])
    E("gpsimd", lambda e: e.dma_start(out=tiny[0:1, T_G11:T_G11 + 1], in_=cc2o_h[:]),
      waits=[("cc", now("cc"))], incs=[("g", 16)])
    g2_g = now("g")
    E("vector", lambda e: e.tensor_tensor(
        tiny[0:1, T_TMP1:T_TMP1 + 1], tiny[0:1, T_G11:T_G11 + 1],
        tiny[0:1, T_SIGD:T_SIGD + 1], ALU.mult), waits=[("g", g2_g)])
    DR("vector")
    E("vector", lambda e: e.tensor_scalar(
        tiny[0:1, T_SD:T_SD + 1], tiny[0:1, T_TMP1:T_TMP1 + 1], 1.0 / QMAX, None, ALU.mult))
    DR("vector")
    E("vector", lambda e: e.reciprocal(
        tiny[0:1, T_RX2:T_RX2 + 1], tiny[0:1, T_SD:T_SD + 1]))
    E("vector", lambda e: e.memset(stg_f32[0:1, 0:128], 1.0), incs=[("v", 1)])
    E("tensor", lambda e: e.matmul(
        PS[0:128, 3584:3585], stg_f32[0:1, 0:128], tiny[0:1, T_RX2:T_RX2 + 1],
        start=True, stop=True), waits=[("v", now("v"))])
    E("tensor", lambda e: e.matmul(
        PS[0:128, 3585:3586], stg_f32[0:1, 0:128], tiny[0:1, T_SD:T_SD + 1],
        start=True, stop=True), incs=[("t", 1)])
    E("vector", lambda e: e.tensor_copy(rd_col[:, :], PS[:, 3584:3585]),
      waits=[("t", now("t"))])
    E("vector", lambda e: e.tensor_copy(sd_col[:, :], PS[:, 3585:3586]), incs=[("v", 1)])
    bar2_v = now("v")
    E("scalar", lambda e: _load_act_set(e, AF.Exp), incs=[("s", 1)])

    hp_stored = now("dhp")
    DOWNP = {0: (0, 512), 1: (1024, 1536)}
    down_free = {0: ("v", 0), 1: ("v", 0)}
    hpin_free = {0: ("v", 0), 1: ("v", 0)}
    xlb_free = {0: ("v", bar2_v), 1: ("v", bar2_v)}
    ybuf_free = {0: ("dout", 0), 1: ("dout", 0)}

    for j in range(NJ):
        E("sync", lambda e, j=j: e.dma_start(
            out=wdT_sb[:, j * 1024:(j + 1) * 1024], in_=wdT_h[j, :, :]),
          waits=[("dwst", prep_dwst_ref[0]), ("v", bar1_v)] if j == 0 else (), incs=[("dwg", 16)])
    wdT_g = now("dwg")

    hq_v = {}
    hq_jv = {}
    hqfree = {0: ("t", 0), 1: ("t", 0)}
    hqbufs = {0: hqT, 1: hqT2}
    qscr_free = {0: ("v", bar2_v), 1: ("v", bar2_v)}

    def emit_quant(tg):
        hqb = hqbufs[tg % 2]
        for j in range(NJ):
            hb = j % 2
            for hh in range(2):
                w = []
                if hh == 0:
                    if tg == 0 and j < 2:
                        w.append(("dhp", hp_stored))
                    w.append(hpin_free[hb])
                E("sync", lambda e, tg=tg, j=j, hh=hh, hb=hb: e.dma_start(
                    out=hp_in[hb][:, hh * 512:(hh + 1) * 512],
                    in_=hp_h[2 * tg + hh, j, :, :]),
                  waits=w, incs=[("dhl", 16)])
            dhn = now("dhl")
            qb = j % 2
            E("vector", lambda e, hb=hb, qb=qb: e.tensor_scalar(
                q_scr[qb][:, :], hp_in[hb][:, :], rd_col[:, 0:1], None, ALU.mult),
              waits=[("dhl", dhn), ("v", bar2_v)], incs=[("v", 1)])
            hpin_free[hb] = ("v", now("v"))
            DR("vector")
            E("vector", lambda e, qb=qb: e.tensor_scalar(
                q_scr[qb][:, :], q_scr[qb][:, :], RC, RC, ALU.add, ALU.subtract))
            DR("vector")
            E("vector", lambda e, j=j, qb=qb, hqb=hqb: e.tensor_scalar(
                hqb[:, j * 1024:(j + 1) * 1024], q_scr[qb][:, :], QMAX, -QMAX - 1.0,
                ALU.min, ALU.max),
              waits=[hqfree[tg % 2]] if j == 0 else (), incs=[("v", 1)])
            hq_jv[(tg, j)] = now("v")
        hq_v[tg] = now("v")

    emit_quant(0)
    for tg in range(8):
        hqb = hqbufs[tg % 2]
        for k in range(8):
            c = tg * 8 + k
            seq, pos = c // 16, (c % 16) * 128
            lb = k % 2
            E("scalar", lambda e, c=c, lb=lb: e.dma_start(
                out=xl_b[lb][:, :], in_=xln1_h[c, :, :]),
              waits=[xlb_free[lb]], incs=[("dlx", 16)])
            dxl = now("dlx")
            pb = k % 2
            o0, o1 = DOWNP[pb]
            for j in range(NJ):
                E("tensor", lambda e, j=j, k=k, o0=o0, hqb=hqb: e.matmul(
                    PS[:, o0:o0 + 512], hqb[:, j * 1024 + k * 128: j * 1024 + (k + 1) * 128],
                    wdT_sb[:, j * 1024: j * 1024 + 512],
                    start=(j == 0), stop=(j == NJ - 1)),
                  waits=(([("v", hq_jv[(tg, j)] if k == 0 else hq_v[tg]),
                           ("dwg", wdT_g), down_free[pb]] if j == 0 else
                          ([("v", hq_jv[(tg, j)])] if k == 0 else ()))))
                E("tensor", lambda e, j=j, k=k, o1=o1, hqb=hqb: e.matmul(
                    PS[:, o1:o1 + 512], hqb[:, j * 1024 + k * 128: j * 1024 + (k + 1) * 128],
                    wdT_sb[:, j * 1024 + 512: (j + 1) * 1024],
                    start=(j == 0), stop=(j == NJ - 1)),
                  incs=[("t", 1)] if j == NJ - 1 else ())
            dt = now("t")
            if k == 0 and tg + 1 < 8:
                emit_quant(tg + 1)
            for fh in range(2):
                off = o0 if fh == 0 else o1
                E("vector", lambda e, fh=fh, off=off, lb=lb: e.scalar_tensor_tensor(
                    xres[:, fh * 512:(fh + 1) * 512], PS[:, off:off + 512],
                    sd_col[:, 0:1], xl_b[lb][:, fh * 512:(fh + 1) * 512],
                    ALU.mult, ALU.add,
                    accum_out=tiny[:, T_S1A + fh:T_S1A + fh + 1]),
                  waits=[("t", dt), ("dlx", dxl)] if fh == 0 else (),
                  incs=[("v", 1)] if fh == 1 else ())
            down_free[pb] = ("v", now("v"))
            xlb_free[lb] = ("v", now("v"))
            DR("vector")
            E("vector", lambda e: e.tensor_tensor(
                tiny[:, T_S1:T_S1 + 1], tiny[:, T_S1A:T_S1A + 1],
                tiny[:, T_S1B:T_S1B + 1], ALU.add))
            E("vector", lambda e: e.scalar_tensor_tensor(
                xq_tm[:, :], xres[:, :], 1.0, xres[:, :], ALU.mult, ALU.mult,
                accum_out=tiny[:, T_S2:T_S2 + 1]))
            DR("vector")
            E("vector", lambda e: e.tensor_scalar(
                tiny[:, T_NEGMEAN:T_NEGMEAN + 1], tiny[:, T_S1:T_S1 + 1], -1.0 / D, None, ALU.mult))
            E("vector", lambda e: e.tensor_scalar(
                tiny[:, T_VP:T_VP + 1], tiny[:, T_S2:T_S2 + 1], 1.0 / D, EPS, ALU.mult, ALU.add))
            DR("vector")
            E("vector", lambda e: e.tensor_tensor(
                tiny[:, T_TMP1:T_TMP1 + 1], tiny[:, T_NEGMEAN:T_NEGMEAN + 1],
                tiny[:, T_NEGMEAN:T_NEGMEAN + 1], ALU.mult))
            DR("vector")
            E("vector", lambda e: e.tensor_tensor(
                tiny[:, T_VP:T_VP + 1], tiny[:, T_VP:T_VP + 1],
                tiny[:, T_TMP1:T_TMP1 + 1], ALU.subtract), incs=[("v", 1)])
            E("scalar", lambda e: e.activation(
                tiny[:, T_TMP2:T_TMP2 + 1], tiny[:, T_VP:T_VP + 1], AF.Ln),
              waits=[("v", now("v"))])
            DR("scalar")
            E("scalar", lambda e: e.activation(
                tiny[:, T_RSTD:T_RSTD + 1], tiny[:, T_TMP2:T_TMP2 + 1], AF.Exp, scale=-0.5),
              incs=[("s", 1)])
            yb = k % 2
            E("vector", lambda e, yb=yb: e.tensor_scalar(
                y_buf[yb][:, :], xres[:, :],
                tiny[:, T_NEGMEAN:T_NEGMEAN + 1], tiny[:, T_RSTD:T_RSTD + 1],
                ALU.add, ALU.mult),
              waits=[("s", now("s")), ybuf_free[yb]], incs=[("v", 1)])
            E("gpsimd", lambda e, yb=yb, seq=seq, pos=pos: e.dma_start(
                out=y_h[seq, pos:pos + 128, :], in_=y_buf[yb][:, :]),
              waits=[("v", now("v"))], incs=[("dout", 16)])
            ybuf_free[yb] = ("dout", now("dout"))
            if debug and tg == 0 and k == 0:
                E("sync", lambda e: e.dma_start(out=dbg["d_xres2"][:, :], in_=xres[:, :]),
                  waits=[("v", now("v"))], incs=[("dmsc", 16)])
                E("sync", lambda e, yb=yb: e.dma_start(out=dbg["d_y"][:, :], in_=y_buf[yb][:, :]),
                  waits=[("v", now("v"))], incs=[("dmsc", 16)])
                for eng in ("vector", "scalar", "tensor", "sync"):
                    sch.wait(eng, ("dmsc", now("dmsc")))
        hqfree[tg % 2] = ("t", now("t"))
    sch.wait("sync", ("dout", now("dout")))
    sch.wait("gpsimd", ("dout", now("dout")))

    block = es.enter_context(nc.Block())
    sch.replay(block)
    return nc


def kernel(**inputs):
    X = np.ascontiguousarray(np.asarray(inputs["L0_attn_X_embed"], dtype=np.float32))
    nc = build_program()
    full = {k2: np.ascontiguousarray(np.asarray(inputs[k1], dtype=np.float32))
            for k1, k2 in (
                ("L0_attn_W_q", "wq"), ("L0_attn_b_q", "bq"),
                ("L0_attn_W_k", "wk"), ("L0_attn_b_k", "bk"),
                ("L0_attn_W_v", "wv"), ("L0_attn_b_v", "bv"),
                ("L0_attn_W_o", "wo"), ("L0_attn_b_o", "bo"),
                ("L0_ffn_clip_x", "cx"), ("L0_ffn_clip_down", "cd"),
                ("L0_ffn_w_gate", "wg"), ("L0_ffn_w_up", "wu"),
                ("L0_ffn_w_down", "wd"))}
    in_maps = []
    for c in range(NCORES):
        m = dict(full)
        m["x"] = np.ascontiguousarray(X[SEQ * c:SEQ * (c + 1)])
        in_maps.append(m)
    res = run_bass_kernel_spmd(nc, in_maps, core_ids=list(range(NCORES)))
    return np.concatenate([r["y"] for r in res.results], axis=0)


